# version 07
# baseline (speedup 1.0000x reference)
import numpy as np
import ml_dtypes
import concourse.bass as bass
import concourse.mybir as mybir
from concourse.bass_utils import run_bass_kernel_spmd

F32 = mybir.dt.float32
BF16 = mybir.dt.bfloat16
I32 = mybir.dt.int32
OP = mybir.AluOpType
AF = mybir.ActivationFunctionType
AX = mybir.AxisListType
bfdt = ml_dtypes.bfloat16

BC = 2
S = 2048
Fd = 2048
NHEAD = 32
NMLP = 16384
SCALE = float(np.sqrt(32.0))
FC = 16
JC = 16
NSUP = 4
TBS = 4
HP = 16
TCH = 16
TG = 8
NCH = 64
TOK = BC * S
GC0 = 1.5957691216057308
GC1 = 2.0 * 0.7978845608028654 * 0.044715

ENGINES = ("tensor", "vector", "scalar", "gpsimd", "sync")
DTSZ = {F32: 4, BF16: 2, I32: 4}


class KB:

    def __init__(self, nc):
        self.nc = nc
        self.stream = {e: [] for e in ENGINES}
        self.sem_h = {}
        self.sem_count = {}
        self.eng_waited = {e: {} for e in ENGINES}
        self.res = {}
        self.phase = 0
        self.esem = {}
        self.new_phase()

    def _sem(self, name):
        if name not in self.sem_h:
            self.sem_h[name] = self.nc.alloc_semaphore(name=name)
            self.sem_count[name] = 0
        return self.sem_h[name]

    def new_phase(self):
        self.phase += 1
        self.esem = {e: f"E{self.phase % 3}_{e}" for e in ENGINES}
        for e in ENGINES:
            self._sem(self.esem[e])

    def _wait(self, eng, semname, val):
        if val <= 0:
            return
        assert val <= self.sem_count[semname], (semname, val, self.sem_count[semname])
        w = self.eng_waited[eng]
        if w.get(semname, 0) >= val:
            return
        w[semname] = val
        h = self.sem_h[semname]
        self.stream[eng].append(lambda e, h=h, v=val: e.wait_ge(h, v))

    def _deps(self, eng, reads, writes):
        for k in reads:
            rec = self.res.get(k)
            if rec and rec["w"]:
                self._wait(eng, *rec["w"])
        for k in writes:
            rec = self.res.get(k)
            if rec:
                if rec["w"]:
                    self._wait(eng, *rec["w"])
                for (sn, v) in rec["r"]:
                    self._wait(eng, sn, v)

    def _record(self, token, reads, writes):
        for k in writes:
            self.res[k] = {"w": token, "r": []}
        for k in reads:
            self.res.setdefault(k, {"w": None, "r": []})["r"].append(token)

    def op(self, eng, fn, reads=(), writes=()):
        self._deps(eng, reads, writes)
        semname = self.esem[eng]
        self.sem_count[semname] += 1
        h = self.sem_h[semname]

        def emit(e, fn=fn, h=h):
            fn(e).then_inc(h, 1)
        self.stream[eng].append(emit)
        token = (semname, self.sem_count[semname])
        self._record(token, reads, writes)
        return token

    def dma(self, queue, out, in_, sname, reads=(), writes=(), **kw):
        self._deps(queue, reads, writes)
        semname = "D_" + sname
        self._sem(semname)
        self._wait(queue, semname, self.sem_count[semname])
        self.sem_count[semname] += 16
        h = self.sem_h[semname]

        def emit(e, out=out, in_=in_, h=h, kw=kw):
            e.dma_start(out=out, in_=in_, **kw).then_inc(h, 16)
        self.stream[queue].append(emit)
        token = (semname, self.sem_count[semname])
        self._record(token, reads, writes)
        return token

    def full_barrier(self):
        counts = [(n, c) for n, c in self.sem_count.items() if c > 0]
        for e in ENGINES:
            for n, c in counts:
                self._wait(e, n, c)
        self.res = {}

    def finalize(self, done_scratch):
        done = self._sem("DONE")
        for e in ("vector", "scalar", "gpsimd"):
            def emit(en, d=done, scr=done_scratch):
                en.memzero(scr).then_inc(d, 1)
            self.stream[e].append(emit)
        self.sem_count["DONE"] = 3
        for name, cnt in list(self.sem_count.items()):
            if cnt > 0 and name != "DONE":
                self._wait("sync", name, cnt)
        self.stream["sync"].append(lambda e, h=done: e.wait_ge(h, 3))
        handles = list(self.sem_h.values())

        def clear(e, hs=handles):
            for hh in hs:
                e.sem_clear(hh)
        self.stream["sync"].append(clear)


def gu_row_start(c):
    j = c % 4
    return 512 * (c // 4) + (j % 2) * 256 + (j // 2)


def build_nc():
    nc = bass.Bass("TRN2", target_bir_lowering=False)

    x_in = nc.dram_tensor("L0_attn_x", [BC, S, Fd], F32, kind="ExternalInput").ap()
    Wq = nc.dram_tensor("L0_attn_Wq", [Fd, Fd], F32, kind="ExternalInput").ap()
    bq = nc.dram_tensor("L0_attn_bq", [Fd], F32, kind="ExternalInput").ap()
    Wk = nc.dram_tensor("L0_attn_Wk", [Fd, Fd], F32, kind="ExternalInput").ap()
    bk = nc.dram_tensor("L0_attn_bk", [Fd], F32, kind="ExternalInput").ap()
    Wv = nc.dram_tensor("L0_attn_Wv", [Fd, Fd], F32, kind="ExternalInput").ap()
    bv = nc.dram_tensor("L0_attn_bv", [Fd], F32, kind="ExternalInput").ap()
    Wo = nc.dram_tensor("L0_attn_Wo", [Fd, Fd], F32, kind="ExternalInput").ap()
    bo = nc.dram_tensor("L0_attn_bo", [Fd], F32, kind="ExternalInput").ap()
    gpk = nc.dram_tensor("L0_ffn_gate_W_packed", [NMLP, Fd // 2], I32, kind="ExternalInput").ap()
    gsc = nc.dram_tensor("L0_ffn_gate_scales", [NMLP, Fd // 32], F32, kind="ExternalInput").ap()
    upk = nc.dram_tensor("L0_ffn_up_W_packed", [NMLP, Fd // 2], I32, kind="ExternalInput").ap()
    usc = nc.dram_tensor("L0_ffn_up_scales", [NMLP, Fd // 32], F32, kind="ExternalInput").ap()
    dpk = nc.dram_tensor("L0_ffn_down_W_packed", [Fd, NMLP // 2], I32, kind="ExternalInput").ap()
    dsc = nc.dram_tensor("L0_ffn_down_scales", [Fd, NMLP // 32], F32, kind="ExternalInput").ap()
    out = nc.dram_tensor("out", [BC, S, Fd], F32, kind="ExternalOutput").ap()

    lnT_d = nc.dram_tensor("lnT_d", [Fd, S], BF16, kind="Internal").ap()
    kT_d = nc.dram_tensor("kT_d", [JC, 128, S], BF16, kind="Internal").ap()
    x1_d = nc.dram_tensor("x1_d", [TOK, Fd], F32, kind="Internal").ap()
    x2p_d = nc.dram_tensor("x2p_d", [TOK, Fd], F32, kind="Internal").ap()
    wguT_d = nc.dram_tensor("wguT_d", [128, 128, 2 * Fd], BF16, kind="Internal").ap()
    wqkvo_d = nc.dram_tensor("wqkvo_d", [4, Fd, Fd], BF16, kind="Internal").ap()
    wqkT_d = nc.dram_tensor("wqkT_d", [2, JC, 128, Fd], BF16, kind="Internal").ap()
    wdT_d = nc.dram_tensor("wdT_d", [128, 128, Fd], BF16, kind="Internal").ap()

    ident = nc.inline_tensor(np.eye(128, dtype=bfdt), "ident")
    ones_row = nc.inline_tensor(np.ones((1, 128), dtype=bfdt), "ones_row")
    E_np = np.zeros((JC, 128, NHEAD), dtype=bfdt)
    for jc in range(JC):
        E_np[jc, 0:64, 2 * jc] = 1
        E_np[jc, 64:128, 2 * jc + 1] = 1
    E_d = nc.inline_tensor(E_np.reshape(JC * 128, NHEAD), "E_ind")

    def sb(name, shape, dtype):
        return nc.alloc_sbuf_tensor(name, shape, dtype)

    ident_s = sb("ident_s", [128, 128], BF16)
    ones_s = sb("ones_s", [1, 128], BF16)
    E_s = sb("E_s", [128, JC * NHEAD], BF16)
    bq_t = sb("bq_t", [128, JC], F32)
    bk_t = sb("bk_t", [128, JC], F32)

    Brow = sb("Brow", [1, 32], BF16)
    Bcol = sb("Bcol", [128, 1], BF16)
    eps_c = sb("eps_c", [128, 1], F32)
    stats = sb("stats", [128, 24], F32)
    negBb = sb("negBb", [128, NHEAD], F32)
    ZtR = sb("ZtR", [128, 128], F32)
    mkq = sb("mkq", [128, 8], F32)
    done_scr = sb("done_scr", [1, 1], F32)
    DQC = 512
    stage = [sb(f"stage{i}", [128, 2048], BF16) for i in range(4)]
    xf = [sb(f"xf{i}", [128, Fd], F32) for i in range(2)]
    o1 = sb("o1", [128, Fd], F32)
    p1 = sb("p1", [128, 8192], BF16)
    wp = sb("wp", [128, 8192], BF16)

    p1a = p1[:, 0:2048]
    p1b = p1[:, 2048:4096]
    pex = [p1[:, 4096 + 512 * i:4096 + 512 * i + 512] for i in range(8)]
    PXK = [f"px{i}" for i in range(8)]
    p1d = p1[:, 6144:8192]
    P1DK = ["px4", "px5", "px6", "px7"]
    bvrow = p1[0:1, 4096:6144]
    borow = p1[0:1, 4096:6144]
    BRK = ["px0", "px1", "px2", "px3"]

    wslot = [wp[:, 2048 * i:2048 * i + 2048] for i in range(4)]

    def qkeys(i):
        return [f"Q{i}{c}" for c in "abcd"]

    wvbuf = [wp[:, 4096:4608], wp[:, 6144:6656], wp[:, 4608:5120], wp[:, 6656:7168]]
    WVK = [["Q2a"], ["Q3a"], ["Q2b"], ["Q3b"]]
    wgu = [wp[:, 0:4096], wp[:, 4096:8192]]
    WGUK = [qkeys(0) + qkeys(1), qkeys(2) + qkeys(3)]
    wdn = [wp[:, 0:1024], wp[:, 2048:3072], wp[:, 4096:5120], wp[:, 6144:7168]]
    WDNK = [["Q0a", "Q0b"], ["Q1a", "Q1b"], ["Q2a", "Q2b"], ["Q3a", "Q3b"]]

    ZONE = nc.sbuf_base + 32
    ZEND = nc.sbuf_top

    def at(name, shape, dtype, off):
        h = nc.alloc_sbuf_tensor_at(name, shape, dtype, offset=off)
        return h, off + int(np.prod(shape[1:])) * DTSZ[dtype]

    az = ZONE
    pkb = []
    for i in range(2):
        h, az = at(f"pkb{i}", [128, 512], I32, az)
        pkb.append(h)
    dqt = []
    for i in range(2):
        h, az = at(f"dqt{i}", [128, DQC], I32, az)
        dqt.append(h)
    wnat = []
    for i in range(2):
        h, az = at(f"wnat{i}", [128, 512], BF16, az)
        wnat.append(h)
    scl = []
    for i in range(2):
        h, az = at(f"scl{i}", [128, 64], F32, az)
        scl.append(h)
    vb, az = at("vb", [128, 16 * NHEAD * 65], BF16, az)
    qTb, az = at("qTb", [128, JC * 512], BF16, az)
    fUT, az = at("fUT", [128, HP * 512], BF16, az)
    lnTs2, az = at("lnTs2", [128, FC * 512], BF16, az)
    kTs_t, az = at("kTs", [128, 2 * S], BF16, az)
    assert az <= ZEND, f"attention zone overflow: {az} > {ZEND}"
    fz = ZONE
    interT, fz = at("interT", [128, NCH * 512], BF16, fz)
    ln2Ts, fz = at("ln2Ts", [128, FC * 512], BF16, fz)
    assert fz <= ZEND, f"ffn zone overflow: {fz} > {ZEND}"

    lnTs = fUT
    kTs = [kTs_t[:, 0:2048], kTs_t[:, 2048:4096]]

    kb = KB(nc)

    psg = [nc.psum_tensor(f"pb{i}", [128, 512], F32) for i in range(8)]
    pst = [g.__enter__() for g in psg]

    def pf(i, n=512):
        return pst[i][:, 0:n]

    def pbf(i):
        return pst[i][:].bitcast(BF16)

    def pk(i):
        return f"pb{i}"

    kb.dma("sync", bass.AP(ident_s, 0, [[128, 128], [1, 128]]), ident.ap(), "pre",
           writes=["ident"])
    kb.dma("sync", bass.AP(ones_s, 0, [[128, 1], [1, 128]]), ones_row.ap(), "pre",
           writes=["ones"])
    kb.dma("sync", E_s[:].rearrange("p (jc h) -> p jc h", jc=JC),
           E_d.ap().rearrange("(jc p) h -> p jc h", p=128), "pre", writes=["E"])
    for jc in range(JC):
        kb.dma("sync", bq_t[:, jc:jc + 1],
               bq[128 * jc:128 * jc + 128].rearrange("(p one) -> p one", one=1),
               "pre", writes=["bq"])
        kb.dma("sync", bk_t[:, jc:jc + 1],
               bk[128 * jc:128 * jc + 128].rearrange("(p one) -> p one", one=1),
               "pre", writes=["bk"])
    kb.op("gpsimd", lambda e: e.memset(eps_c[:], 1e-5), writes=["eps"])
    ones_view = bass.AP(vb, 64, [[16 * NHEAD * 65, 128], [NHEAD * 65, 16], [65, NHEAD]])
    kb.op("gpsimd", lambda e: e.memset(ones_view, 1.0), writes=["vbones"])
    for m, Wsrc in ((1, Wk), (2, Wv), (0, Wq), (3, Wo)):
        role = {1: 0, 0: 1}.get(m)
        for fc in range(FC):
            qslot = fc % 4
            buf = wslot[qslot]
            kb.dma("gpsimd", buf, Wsrc[128 * fc:128 * fc + 128, :],
                   f"pcin{qslot}", writes=qkeys(qslot))
            if role is not None:
                kb.dma("sync",
                       wqkT_d[role][:, :, 128 * fc:128 * fc + 128].rearrange(
                           "jc p j -> p jc j"),
                       buf.rearrange("p (jc j) -> p jc j", jc=JC),
                       f"pcout{qslot}", reads=qkeys(qslot),
                       writes=[f"wb_{role}_{fc}"])
            else:
                kb.dma("sync", wqkvo_d[m, 128 * fc:128 * fc + 128, :], buf,
                       f"pcout{qslot}", reads=qkeys(qslot), writes=[f"wc_{m}_{fc}"])

    P126 = float(2.0 ** 126)

    def dq_nibble(pap, pkkey, sckey, sbc, dst_ap, dstkey, lohi):
        sh = 28 if lohi == 0 else 24
        n = pap.shape[1]
        t = dqt[lohi]
        tk = f"dqt{lohi}"
        kb.op("vector", lambda e: e.tensor_scalar(
            t[:, 0:n], pap, sh, 6, OP.logical_shift_left, OP.arith_shift_right),
            reads=[pkkey], writes=[tk])
        kb.op("vector", lambda e: e.tensor_scalar(
            t[:, 0:n], t[:, 0:n], 0x81C00000 - 2 ** 32, 0, OP.bitwise_and, OP.bitwise_or),
            reads=[tk], writes=[tk])
        kb.op("gpsimd", lambda e: e.tensor_tensor(
            dst_ap.rearrange("p (b i) -> p b i", i=16),
            t[:, 0:n].bitcast(F32).rearrange("p (b i) -> p b i", i=16),
            sbc, OP.mult),
            reads=[tk, sckey], writes=[dstkey])

    dq_state = {"i": 0, "banks": (0, 1)}

    def emit_gu_slab_dve(c):
        rs = gu_row_start(c)
        for half, (pk_src, sc_src) in enumerate(((gpk, gsc), (upk, usc))):
            wns = stage[2 + half]
            wnkey = f"stage{2 + half}"
            sbuf_ = scl[half]
            sckey = f"dqsc{half}"
            kb.dma("gpsimd", sbuf_[:], sc_src[rs:rs + 255:2, :], f"dqscs{half}",
                   writes=[sckey])
            kb.op("vector", lambda e, s=sbuf_: e.tensor_scalar(
                s[:], s[:], P126, 1.0, OP.mult, OP.mult),
                reads=[sckey], writes=[sckey])
            for pl in range(2):
                pbuf = pkb[pl]
                pkkey = f"dqpk{pl}"
                kb.dma("gpsimd", pbuf[:],
                       pk_src[rs:rs + 255:2, 512 * pl:512 * pl + 512],
                       f"dqpks{pl}", writes=[pkkey])
                pap = pbuf[:]
                sbc = sbuf_[:, 32 * pl:32 * pl + 32].unsqueeze(2).broadcast_to(
                    [128, 32, 16])
                for lohi in range(2):
                    dq_nibble(pap, pkkey, sckey, sbc,
                              wns[:, 1024 * lohi + 512 * pl:
                                  1024 * lohi + 512 * pl + 512], wnkey, lohi)

    def emit_gu_slab_pe(c):
        for half in range(2):
            wns = stage[2 + half]
            wnkey = f"stage{2 + half}"
            stg = stage[half]
            stgkey = f"stage{half}"
            bank = dq_state["banks"][half]
            for grp in range(4):
                for sub in range(4):
                    fcid = 4 * grp + sub
                    kb.op("tensor", lambda e, fcid=fcid, sub=sub, wns=wns, bank=bank:
                          e.transpose(
                              pbf(bank)[:, 128 * sub:128 * sub + 128],
                              wns[:, 128 * fcid:128 * fcid + 128], ident_s[:]),
                          reads=[wnkey, "ident"], writes=[pk(bank)])
                kb.op("vector", lambda e, grp=grp, stg=stg, bank=bank: e.tensor_copy(
                    stg[:, 512 * grp:512 * grp + 512], pbf(bank)[:, 0:512]),
                    reads=[pk(bank)], writes=[stgkey])
            kb.dma("scalar", wguT_d[c, :, 2048 * half:2048 * half + 2048], stg[:],
                   "wguT_out", reads=[stgkey], writes=[f"wguT_{c}"])

    def emit_down_unit(u):
        pend = []

        def flush():
            for f in pend:
                f()
            pend.clear()

        for hc in range(16):
            pbuf = pkb[hc % 2]
            pkkey = f"dqpk{hc % 2}"
            kb.dma("gpsimd", pbuf[:, 0:256],
                   dpk[128 * hc:128 * hc + 128, 256 * u:256 * u + 256],
                   f"dqpks{hc % 2}", writes=[pkkey])
            sbuf_ = scl[hc % 2]
            sckey = f"dqsc{hc % 2}"
            kb.dma("gpsimd", sbuf_[:, 0:16],
                   dsc[128 * hc:128 * hc + 128, 16 * u:16 * u + 16],
                   f"dqscs{hc % 2}", writes=[sckey])
            kb.op("vector", lambda e, s=sbuf_: e.tensor_scalar(
                s[:, 0:16], s[:, 0:16], P126, 1.0, OP.mult, OP.mult),
                reads=[sckey], writes=[sckey])
            wn = wnat[hc % 2]
            wnk = f"dqwn{hc % 2}"
            pap = pbuf[:, 0:256]
            sbc = sbuf_[:, 0:16].unsqueeze(2).broadcast_to([128, 16, 16])
            for lohi in range(2):
                dq_nibble(pap, pkkey, sckey, sbc,
                          wn[:, 256 * lohi:256 * lohi + 256], wnk + str(lohi), lohi)
            flush()

            def pe_part(hc=hc, wn=wn, wnk=wnk):
                for lohi in range(2):
                    for sub in range(2):
                        sl = 2 * lohi + sub
                        bank = dq_state["banks"][lohi]
                        kb.op("tensor", lambda e, wn=wn, lohi=lohi, sub=sub, bank=bank:
                              e.transpose(
                                  pbf(bank)[:, 0:128],
                                  wn[:, 256 * lohi + 128 * sub:
                                     256 * lohi + 128 * sub + 128],
                                  ident_s[:]),
                              reads=[wnk + str(lohi), "ident"], writes=[pk(bank)])
                        kb.op("vector", lambda e, hc=hc, sl=sl, bank=bank: e.tensor_copy(
                            stage[sl][:, 128 * hc:128 * hc + 128], pbf(bank)[:, 0:128]),
                            reads=[pk(bank)], writes=[f"stage{sl}"])
            pend.append(pe_part)
        flush()
        for sl in range(4):
            kb.dma("scalar", wdT_d[4 * u + sl, :, :], stage[sl][:], "wdT_out",
                   reads=[f"stage{sl}"], writes=[f"wdT_{4 * u + sl}"])

    dq_dve = [lambda c=c: emit_gu_slab_dve(c) for c in range(128)]
    dq_pe = [lambda c=c: emit_gu_slab_pe(c) for c in range(128)]
    dq_down = [lambda u=u: emit_down_unit(u) for u in range(32)]
    N_GU, N_DOWN = 128, 32
    N_UNITS = N_GU + 1 + N_DOWN

    def dq_step(n=1, banks=None):
        if banks is not None:
            dq_state["banks"] = banks
        for _ in range(n):
            i = dq_state["i"]
            if i >= N_UNITS:
                return
            if 0 < i <= N_GU:
                dq_pe[i - 1]()
            if i < N_GU:
                dq_dve[i]()
            if i > N_GU:
                dq_down[i - N_GU - 1]()
            dq_state["i"] += 1

    def ln_from_sums(csum, csumsq, cs1, cs2):
        t1 = stats[:, cs1:cs1 + 1]
        t2 = stats[:, cs2:cs2 + 1]
        kb.op("vector", lambda e: e.tensor_scalar(
            stats[:, csum:csum + 1], stats[:, csum:csum + 1], 1.0 / Fd, 1.0,
            OP.mult, OP.mult), reads=[f"st{csum}"], writes=[f"st{csum}"])
        kb.op("vector", lambda e: e.scalar_tensor_tensor(
            t1, stats[:, csum:csum + 1], 1.0, stats[:, csum:csum + 1],
            OP.mult, OP.mult), reads=[f"st{csum}"], writes=[f"st{cs1}"])
        kb.op("vector", lambda e: e.tensor_scalar(
            stats[:, csumsq:csumsq + 1], stats[:, csumsq:csumsq + 1], 1.0 / Fd, 1e-5,
            OP.mult, OP.add), reads=[f"st{csumsq}"], writes=[f"st{csumsq}"])
        kb.op("vector", lambda e: e.tensor_tensor(
            stats[:, csumsq:csumsq + 1], stats[:, csumsq:csumsq + 1], t1,
            OP.subtract), reads=[f"st{csumsq}", f"st{cs1}"], writes=[f"st{csumsq}"])
        kb.op("vector", lambda e: e.tensor_scalar(
            t1.bitcast(I32), stats[:, csumsq:csumsq + 1].bitcast(I32), 1, 0,
            OP.logical_shift_right, OP.bitwise_or),
            reads=[f"st{csumsq}"], writes=[f"st{cs1}"])
        kb.op("vector", lambda e: e.tensor_scalar(
            t1.bitcast(I32), t1.bitcast(I32), -1, 0x5f3759df, OP.mult, OP.add),
            reads=[f"st{cs1}"], writes=[f"st{cs1}"])
        for _ in range(2):
            kb.op("vector", lambda e: e.scalar_tensor_tensor(
                t2, t1, 1.0, t1, OP.mult, OP.mult),
                reads=[f"st{cs1}"], writes=[f"st{cs2}"])
            kb.op("vector", lambda e: e.tensor_tensor(
                t2, t2, stats[:, csumsq:csumsq + 1], OP.mult),
                reads=[f"st{cs2}", f"st{csumsq}"], writes=[f"st{cs2}"])
            kb.op("vector", lambda e: e.tensor_scalar(
                t2, t2, -0.5, 1.5, OP.mult, OP.add),
                reads=[f"st{cs2}"], writes=[f"st{cs2}"])
            kb.op("vector", lambda e: e.tensor_tensor(
                t1, t1, t2, OP.mult),
                reads=[f"st{cs1}", f"st{cs2}"], writes=[f"st{cs1}"])
        kb.op("vector", lambda e: e.tensor_copy(
            stats[:, csumsq:csumsq + 1], t1),
            reads=[f"st{cs1}"], writes=[f"st{csumsq}"])

    def ln_stats(xkey, xap, csum, csumsq, cs1, cs2, junk=None, junkkey=("o1",)):
        if isinstance(xkey, str):
            xkey = [xkey]
        jap = o1[:] if junk is None else junk
        junkkey = list(junkkey)
        kb.op("vector", lambda e: e.scalar_tensor_tensor(
            jap, xap, 1.0, xap, OP.mult, OP.max,
            accum_out=stats[:, csum:csum + 1]),
            reads=xkey, writes=junkkey + [f"st{csum}"])
        kb.op("vector", lambda e: e.scalar_tensor_tensor(
            jap, xap, 1.0, xap, OP.mult, OP.mult,
            accum_out=stats[:, csumsq:csumsq + 1]),
            reads=xkey + junkkey, writes=junkkey + [f"st{csumsq}"])
        ln_from_sums(csum, csumsq, cs1, cs2)

    def ln_tile_transposed(seq, trow, dst, keyfmt, colbase, bufi):
        xkey = f"xf{bufi}"
        kb.dma("sync", xf[bufi][:], x_in[seq, trow:trow + 128, :], f"xfs{bufi}",
               writes=[xkey])
        cb = 4 * bufi
        jk = [("o1",), ("p1b",)][bufi]
        jap = [o1[:], p1b][bufi]
        ln_stats(xkey, xf[bufi][:], cb, cb + 1, cb + 2, cb + 3, jap, jk)
        kb.op("vector", lambda e: e.tensor_scalar(
            p1a, xf[bufi][:], stats[:, cb:cb + 1], stats[:, cb + 1:cb + 2],
            OP.subtract, OP.mult),
            reads=[xkey, f"st{cb}", f"st{cb + 1}"], writes=["p1a"])
        for fc in range(FC):
            bank = 4 + fc % 2
            kb.op("tensor", lambda e, fc=fc, bank=bank: e.transpose(
                pbf(bank)[:, 0:128], p1a[:, 128 * fc:128 * fc + 128], ident_s[:]),
                reads=["p1a", "ident"], writes=[pk(bank)])
            kb.op("vector", lambda e, fc=fc, bank=bank: e.tensor_copy(
                dst[:, 512 * fc + colbase:512 * fc + colbase + 128],
                pbf(bank)[:, 0:128]),
                reads=[pk(bank)], writes=[keyfmt.format(i=fc)])

    def attn_phaseA(seq):
        for sup in range(NSUP):
            kb.new_phase()
            kb.dma("gpsimd", bvrow, bv.unsqueeze(0), "bvrow", writes=BRK)
            for tb in range(TBS):
                ln_tile_transposed(seq, 512 * sup + 128 * tb, lnTs, "fu_{i}",
                                   128 * tb, tb % 2)
            for fc in range(FC):
                kb.dma("sync", lnT_d[128 * fc:128 * fc + 128, 512 * sup:512 * sup + 512],
                       lnTs[:, 512 * fc:512 * fc + 512], "lnT_out",
                       reads=[f"fu_{fc}"], writes=[f"lnTd_{fc}_{sup}"])
            for jc in range(JC):
                sl, slkeys = wslot[jc % 2], qkeys(jc % 2)
                kb.dma("sync", sl, wqkT_d[0, jc], f"wsls{jc % 2}",
                       reads=[f"wb_0_{fc}" for fc in range(FC)], writes=slkeys)
                bank = 6 + jc % 2

                def mm(e, sl=sl, bank=bank):
                    for fc in range(FC):
                        ins = e.matmul(pf(bank), sl[:, 128 * fc:128 * fc + 128],
                                       lnTs[:, 512 * fc:512 * fc + 512],
                                       start=(fc == 0), stop=(fc == FC - 1))
                    return ins
                kb.op("tensor", mm,
                      reads=slkeys + [f"fu_{fc}" for fc in range(FC)],
                      writes=[pk(bank)])
                kt, ktkey = kTs[jc % 2], f"kTs{jc % 2}"
                kb.op("vector", lambda e, bank=bank, jc=jc, kt=kt: e.tensor_scalar(
                    kt[:, 0:512], pf(bank), bk_t[:, jc:jc + 1], 1.0, OP.add, OP.mult),
                    reads=[pk(bank), "bk"], writes=[ktkey])
                kb.dma("sync", kT_d[jc, :, 512 * sup:512 * sup + 512], kt[:, 0:512],
                       "kT_out", reads=[ktkey], writes=[f"kTd_{jc}_{sup}"])
            for hs in range(4):
                for fc in range(FC):
                    ch, chkeys = wvbuf[fc % 4], WVK[fc % 4]
                    kb.dma("sync" if fc % 2 == 0 else "gpsimd", ch,
                           wqkvo_d[2, 128 * fc:128 * fc + 128, 512 * hs:512 * hs + 512],
                           f"wvs{fc % 4}", reads=[f"wc_2_{fc}"], writes=chkeys)

                    def mmv(e, ch=ch, fc=fc):
                        for tb2 in range(4):
                            ins = e.matmul(
                                pf(tb2),
                                lnTs[:, 512 * fc + 128 * tb2:512 * fc + 128 * tb2 + 128],
                                ch, start=(fc == 0), stop=False)
                        return ins
                    kb.op("tensor", mmv,
                          reads=chkeys + [f"fu_{fc2}" for fc2 in range(FC)],
                          writes=[pk(0), pk(1), pk(2), pk(3)])

                def mmvb(e, hs=hs):
                    for tb2 in range(4):
                        ins = e.matmul(pf(tb2), ones_s[0:1, :],
                                       bvrow[0:1, 512 * hs:512 * hs + 512],
                                       start=False, stop=True)
                    return ins
                kb.op("tensor", mmvb, reads=["ones", f"px{hs}"],
                      writes=[pk(0), pk(1), pk(2), pk(3)])
                for tb2 in range(4):
                    tbg = 4 * sup + tb2
                    dst = bass.AP(vb, 2080 * tbg + 65 * 8 * hs,
                                  [[16 * NHEAD * 65, 128], [65, 8], [1, 64]])
                    kb.op("vector", lambda e, tb2=tb2, dst=dst: e.tensor_copy(
                        dst, pf(tb2).rearrange("p (h d) -> p h d", d=64)),
                        reads=[pk(tb2)], writes=[f"vb_{tbg}_{hs}"])
            dq_step(8, banks=(0, 1))
        kb.new_phase()
        for ts4 in range(4):
            for jc in range(JC):
                kt, ktkey = kTs[jc % 2], f"kTs{jc % 2}"
                kb.dma("sync", kt[:, 0:512], kT_d[jc, :, 512 * ts4:512 * ts4 + 512],
                       f"kTss{jc % 2}", reads=[f"kTd_{jc}_{ts4}"], writes=[ktkey])
                kb.op("vector", lambda e, kt=kt: e.tensor_tensor(
                    p1a[:, 0:512], kt[:, 0:512], kt[:, 0:512], OP.mult),
                    reads=[ktkey], writes=["p1a"])
                kb.op("tensor", lambda e, jc=jc: e.matmul(
                    pf(7)[0:32, :], E_s[:, 32 * jc:32 * jc + 32], p1a[:, 0:512],
                    start=(jc == 0), stop=(jc == JC - 1)),
                    reads=["p1a", "E"], writes=[pk(7)])
            kb.op("vector", lambda e: e.tensor_reduce(
                stats[0:32, 16:17], pf(7)[0:32, :], AX.X, OP.max),
                reads=[pk(7)], writes=["st16"])
            if ts4 == 0:
                kb.op("vector", lambda e: e.tensor_copy(
                    mkq[0:32, 0:1], stats[0:32, 16:17]),
                    reads=["st16"], writes=["maxk2"])
            else:
                kb.op("vector", lambda e: e.tensor_tensor(
                    mkq[0:32, 0:1], mkq[0:32, 0:1], stats[0:32, 16:17], OP.max),
                    reads=["st16", "maxk2"], writes=["maxk2"])

    def attn_phaseB(seq):
        for sup in range(NSUP):
            kb.new_phase()
            for fc in range(FC):
                kb.dma("sync", lnTs2[:, 512 * fc:512 * fc + 512],
                       lnT_d[128 * fc:128 * fc + 128, 512 * sup:512 * sup + 512],
                       f"lnT_in{fc % 2}", reads=[f"lnTd_{fc}_{sup}"],
                       writes=[f"l2_{fc}"])
            for hp0 in range(2):
                kb.dma("sync", kTs[hp0], kT_d[hp0, :, :], f"kTss{hp0}",
                       reads=[f"kTd_{hp0}_{s2}" for s2 in range(4)],
                       writes=[f"kTs{hp0}"])
            for jc in range(JC):
                sl, slkeys = wslot[jc % 2], qkeys(jc % 2)
                kb.dma("sync", sl, wqkT_d[1, jc], f"wsls{jc % 2}",
                       reads=[f"wb_1_{fc}" for fc in range(FC)], writes=slkeys)
                bank = 6 + jc % 2

                def mm(e, sl=sl, bank=bank):
                    for fc in range(FC):
                        ins = e.matmul(pf(bank), sl[:, 128 * fc:128 * fc + 128],
                                       lnTs2[:, 512 * fc:512 * fc + 512],
                                       start=(fc == 0), stop=(fc == FC - 1))
                    return ins
                kb.op("tensor", mm,
                      reads=slkeys + [f"l2_{fc}" for fc in range(FC)],
                      writes=[pk(bank)])
                kb.op("vector", lambda e, bank=bank, jc=jc: e.tensor_scalar(
                    qTb[:, 512 * jc:512 * jc + 512], pf(bank),
                    bq_t[:, jc:jc + 1], 1.0, OP.add, OP.mult),
                    reads=[pk(bank), "bq"], writes=[f"qTb_{jc}"])
            for jc in range(JC):
                kb.op("vector", lambda e, jc=jc: e.tensor_tensor(
                    p1a[:, 0:512], qTb[:, 512 * jc:512 * jc + 512],
                    qTb[:, 512 * jc:512 * jc + 512], OP.mult),
                    reads=[f"qTb_{jc}"], writes=["p1a"])
                kb.op("tensor", lambda e, jc=jc: e.matmul(
                    pf(7)[0:32, :], E_s[:, 32 * jc:32 * jc + 32], p1a[:, 0:512],
                    start=(jc == 0), stop=(jc == JC - 1)),
                    reads=["p1a", "E"], writes=[pk(7)])
            kb.op("vector", lambda e: e.tensor_reduce(
                mkq[0:32, 1:2], pf(7)[0:32, :], AX.X, OP.max),
                reads=[pk(7)], writes=["maxq2"])
            kb.op("vector", lambda e: e.tensor_tensor(
                mkq[0:32, 2:3], mkq[0:32, 0:1], mkq[0:32, 1:2], OP.mult),
                reads=["maxk2", "maxq2"], writes=["B2"])
            kb.op("scalar", lambda e: e.activation(
                mkq[0:32, 3:4], mkq[0:32, 2:3], AF.Ln),
                reads=["B2"], writes=["B3"])
            kb.op("scalar", lambda e: e.activation(
                mkq[0:32, 4:5], mkq[0:32, 3:4], AF.Exp, scale=0.5),
                reads=["B3"], writes=["B4"])
            kb.op("vector", lambda e: e.tensor_scalar(
                mkq[0:32, 5:6], mkq[0:32, 4:5], -1.0 / SCALE, 1.0, OP.mult, OP.mult),
                reads=["B4"], writes=["B5"])
            kb.op("vector", lambda e: e.tensor_copy(
                Bcol[0:32, :], mkq[0:32, 5:6]),
                reads=["B5"], writes=["Bcol"])
            kb.op("tensor", lambda e: e.transpose(
                pbf(6)[0:1, 0:32], Bcol[0:32, :], ident_s[0:32, 0:32]),
                reads=["Bcol", "ident"], writes=[pk(6)])
            kb.op("vector", lambda e: e.tensor_copy(
                Brow[0:1, :], pbf(6)[0:1, 0:32]),
                reads=[pk(6)], writes=["Brow"])
            kb.op("tensor", lambda e: e.matmul(
                pf(7)[:, 0:32], ones_s[0:1, :], Brow[0:1, :], start=True, stop=True),
                reads=["Brow", "ones"], writes=[pk(7)])
            kb.op("vector", lambda e: e.tensor_copy(
                negBb[:, :], pf(7)[:, 0:32]),
                reads=[pk(7)], writes=["negBb"])

            dq_step(12, banks=(2, 3))

            for hp in range(HP):
                kt, ktkey = kTs[hp % 2], f"kTs{hp % 2}"
                pend_av = []

                def mk_av(tc, w0=None, w1=None, k0=None, k1=None, hp=hp):
                    def go():
                        vt0 = vb[:, 2080 * tc + 65 * (2 * hp):
                                 2080 * tc + 65 * (2 * hp) + 65]
                        vt1 = vb[:, 2080 * tc + 65 * (2 * hp + 1):
                                 2080 * tc + 65 * (2 * hp + 1) + 65]
                        vkeys = [f"vb_{tc}_{hs}" for hs in range(4)] + ["vbones"]
                        kb.op("tensor", lambda e: e.matmul(
                            pf(4)[0:65, :], vt0, w0, start=(tc == 0),
                            stop=(tc == TCH - 1)),
                            reads=[k0] + vkeys, writes=[pk(4)])
                        kb.op("tensor", lambda e: e.matmul(
                            pf(5)[0:65, :], vt1, w1, start=(tc == 0),
                            stop=(tc == TCH - 1)),
                            reads=[k1] + vkeys, writes=[pk(5)])
                    return go

                for tc in range(TCH):
                    b0, b1 = (0, 1) if tc % 2 == 0 else (2, 3)
                    w0, w1 = pex[(2 * tc) % 8], pex[(2 * tc + 1) % 8]
                    k0, k1 = PXK[(2 * tc) % 8], PXK[(2 * tc + 1) % 8]
                    kb.op("tensor", lambda e, kt=kt, tc=tc, b0=b0, hp=hp: e.matmul(
                        pf(b0), kt[0:64, 128 * tc:128 * tc + 128],
                        qTb[0:64, 512 * hp:512 * hp + 512],
                        start=True, stop=True, tile_position=(0, 0)),
                        reads=[ktkey, f"qTb_{hp}"], writes=[pk(b0)])
                    kb.op("tensor", lambda e, kt=kt, tc=tc, b1=b1, hp=hp: e.matmul(
                        pf(b1), kt[64:128, 128 * tc:128 * tc + 128],
                        qTb[64:128, 512 * hp:512 * hp + 512],
                        start=True, stop=True, tile_position=(64, 0)),
                        reads=[ktkey, f"qTb_{hp}"], writes=[pk(b1)])
                    if len(pend_av) == 2:
                        pend_av.pop(0)()
                    kb.op("scalar", lambda e, w0=w0, hp=hp, b0=b0: e.activation(
                        w0, pf(b0), AF.Exp, bias=negBb[:, 2 * hp:2 * hp + 1],
                        scale=1.0 / SCALE),
                        reads=[pk(b0), "negBb"], writes=[k0])
                    kb.op("scalar", lambda e, w1=w1, hp=hp, b1=b1: e.activation(
                        w1, pf(b1), AF.Exp, bias=negBb[:, 2 * hp + 1:2 * hp + 2],
                        scale=1.0 / SCALE),
                        reads=[pk(b1), "negBb"], writes=[k1])
                    pend_av.append(mk_av(tc, w0=w0, w1=w1, k0=k0, k1=k1))
                for go in pend_av:
                    go()
                pend_av = []
                kb.op("vector", lambda e, hp=hp: e.tensor_copy(
                    fUT[0:64, 512 * hp:512 * hp + 512], pf(4)[0:64, :]),
                    reads=[pk(4)], writes=[f"fu_{hp}"])
                kb.op("vector", lambda e, hp=hp: e.tensor_copy(
                    fUT[64:128, 512 * hp:512 * hp + 512], pf(5)[0:64, :]),
                    reads=[pk(5), f"fu_{hp}"], writes=[f"fu_{hp}"])
                kb.op("vector", lambda e: e.tensor_copy(
                    p1a[64:65, 0:512], pf(4)[64:65, :]),
                    reads=[pk(4)], writes=["p1a"])
                kb.op("vector", lambda e: e.tensor_copy(
                    p1a[64:65, 512:1024], pf(5)[64:65, :]),
                    reads=[pk(5), "p1a"], writes=["p1a"])
                for half in range(2):
                    h = 2 * hp + half
                    for sc in range(4):
                        kb.op("tensor", lambda e, half=half, sc=sc, h=h: e.transpose(
                            pbf(6)[:, 2 * (32 * sc + h):2 * (32 * sc + h) + 1],
                            p1a[64:65, 512 * half + 128 * sc:
                                512 * half + 128 * sc + 128],
                            ident_s[64:65, 64:65]),
                            reads=["p1a", "ident"], writes=[pk(6)])
                if hp + 2 < HP:
                    kb.dma("sync", kTs[hp % 2], kT_d[hp + 2, :, :], f"kTss{hp % 2}",
                           reads=[f"kTd_{hp + 2}_{s2}" for s2 in range(4)],
                           writes=[ktkey])
            kb.op("vector", lambda e: e.reciprocal(
                ZtR[:, 0:128], pbf(6)[:, 0:256:2]),
                reads=[pk(6)], writes=["ZtR"])
            for sc in range(4):
                for hp in range(HP):
                    bank = 6 + hp % 2
                    kb.op("tensor", lambda e, hp=hp, sc=sc, bank=bank: e.transpose(
                        pbf(bank)[:, 0:128],
                        fUT[:, 512 * hp + 128 * sc:512 * hp + 128 * sc + 128],
                        ident_s[:]),
                        reads=[f"fu_{hp}", "ident"], writes=[pk(bank)])
                    for half in range(2):
                        h = 2 * hp + half
                        kb.op("vector", lambda e, h=h, sc=sc, half=half, bank=bank:
                              e.tensor_tensor(
                                  p1d[:, 64 * h:64 * h + 64],
                                  pbf(bank)[:, 64 * half:64 * half + 64],
                                  ZtR[:, 32 * sc + h:32 * sc + h + 1].broadcast_to(
                                      [128, 64]),
                                  OP.mult),
                              reads=[pk(bank), "ZtR"], writes=[PXK[4 + h // 8]])
                for jc in range(JC):
                    bank = 6 + jc % 2
                    kb.op("tensor", lambda e, jc=jc, bank=bank: e.transpose(
                        pbf(bank)[:, 0:128], p1d[:, 128 * jc:128 * jc + 128],
                        ident_s[:]),
                        reads=[PXK[4 + jc // 4], "ident"], writes=[pk(bank)])
                    kb.op("vector", lambda e, jc=jc, sc=sc, bank=bank: e.tensor_copy(
                        lnTs2[:, 512 * jc + 128 * sc:512 * jc + 128 * sc + 128],
                        pbf(bank)[:, 0:128]),
                        reads=[pk(bank)], writes=[f"l2_{jc}"])
            attS = [p1b, p1d]
            attK = [["p1b"], P1DK]
            kb.dma("gpsimd", borow, bo.unsqueeze(0), "borow", writes=BRK)
            for sbp in range(2):
                for jc in range(JC):
                    wo, wokeys = wslot[2 + jc % 2], qkeys(2 + jc % 2)
                    kb.dma("sync" if jc % 2 == 0 else "gpsimd", wo,
                           wqkvo_d[3, 128 * jc:128 * jc + 128, :],
                           f"wos{jc % 2}", reads=[f"wc_3_{jc}"], writes=wokeys)

                    def mmo(e, wo=wo, jc=jc, sbp=sbp):
                        for i in range(2):
                            sb_ = 2 * sbp + i
                            for js in range(4):
                                ins = e.matmul(
                                    pf(2 * js + i),
                                    lnTs2[:, 512 * jc + 128 * sb_:
                                          512 * jc + 128 * sb_ + 128],
                                    wo[:, 512 * js:512 * js + 512],
                                    start=(jc == 0), stop=False)
                        return ins
                    kb.op("tensor", mmo,
                          reads=wokeys + [f"l2_{jc2}" for jc2 in range(JC)],
                          writes=[pk(b) for b in range(8)])

                def mmob(e):
                    for i in range(2):
                        for js in range(4):
                            ins = e.matmul(pf(2 * js + i), ones_s[0:1, :],
                                           borow[0:1, 512 * js:512 * js + 512],
                                           start=False, stop=True)
                    return ins
                kb.op("tensor", mmob, reads=["ones"] + BRK,
                      writes=[pk(b) for b in range(8)])
                for i in range(2):
                    for js in range(4):
                        kb.op("vector", lambda e, i=i, js=js: e.tensor_scalar(
                            attS[i][:, 512 * js:512 * js + 512], pf(2 * js + i),
                            1.0, 1.0, OP.mult, OP.mult),
                            reads=[pk(2 * js + i)], writes=attK[i])
                for i in range(2):
                    sb_ = 2 * sbp + i
                    kb.op("vector", lambda e, i=i: e.scalar_tensor_tensor(
                        o1[:], attS[i], 1.0, attS[i], OP.mult, OP.max,
                        accum_out=stats[:, 16:17]),
                        reads=attK[i], writes=["o1", "st16"])
                    kb.op("vector", lambda e, i=i: e.scalar_tensor_tensor(
                        o1[:], attS[i], 1.0, attS[i], OP.mult, OP.mult,
                        accum_out=stats[:, 17:18]),
                        reads=attK[i] + ["o1"], writes=["o1", "st17"])
                    ln_from_sums(16, 17, 18, 19)
                    kb.op("vector", lambda e: e.scalar_tensor_tensor(
                        stats[:, 20:21], stats[:, 16:17], 1.0, stats[:, 17:18],
                        OP.mult, OP.mult),
                        reads=["st16", "st17"], writes=["st20"])
                    xkey = f"xf{i}"
                    trow = 512 * sup + 128 * sb_
                    kb.dma("sync", xf[i][:], x_in[seq, trow:trow + 128, :],
                           f"xfs{i}", writes=[xkey])
                    kb.op("vector", lambda e, i=i: e.tensor_scalar(
                        o1[:], xf[i][:], stats[:, 20:21], 1.0, OP.subtract, OP.mult),
                        reads=[xkey, "st20"], writes=["o1"])
                    kb.op("vector", lambda e, i=i: e.scalar_tensor_tensor(
                        xf[i][:], attS[i], stats[:, 17:18], o1[:], OP.mult, OP.add),
                        reads=attK[i] + ["st17", "o1"], writes=[xkey])
                    kb.dma("sync", x1_d[2048 * seq + trow:2048 * seq + trow + 128, :],
                           xf[i][:], "x1_out", reads=[xkey],
                           writes=[f"x1d_{(2048 * seq + trow) // 128}"])

    def ffn_tg(tg):
        kb.new_phase()
        rowbase = 512 * tg
        for tb in range(TBS):
            bi = tb % 2
            xkey = f"xf{bi}"
            kb.dma("sync", xf[bi][:], x1_d[rowbase + 128 * tb:rowbase + 128 * tb + 128, :],
                   f"xfs{bi}", reads=[f"x1d_{(rowbase + 128 * tb) // 128}"],
                   writes=[xkey])
            cb = 4 * bi
            jk = [("o1",), ("p1b",)][bi]
            jap = [o1[:], p1b][bi]
            ln_stats(xkey, xf[bi][:], cb, cb + 1, cb + 2, cb + 3, jap, jk)
            kb.op("vector", lambda e, bi=bi, cb=cb: e.tensor_scalar(
                p1a[:, 0:1024], xf[bi][:, 0::2], stats[:, cb:cb + 1],
                stats[:, cb + 1:cb + 2], OP.subtract, OP.mult),
                reads=[xkey, f"st{cb}", f"st{cb + 1}"], writes=["p1a"])
            kb.op("vector", lambda e, bi=bi, cb=cb: e.tensor_scalar(
                p1a[:, 1024:2048], xf[bi][:, 1::2], stats[:, cb:cb + 1],
                stats[:, cb + 1:cb + 2], OP.subtract, OP.mult),
                reads=[xkey, f"st{cb}", f"st{cb + 1}", "p1a"], writes=["p1a"])
            for fc in range(FC):
                bank = 4 + fc % 2
                kb.op("tensor", lambda e, fc=fc, bank=bank: e.transpose(
                    pbf(bank)[:, 0:128], p1a[:, 128 * fc:128 * fc + 128], ident_s[:]),
                    reads=["p1a", "ident"], writes=[pk(bank)])
                kb.op("vector", lambda e, fc=fc, tb=tb, bank=bank: e.tensor_copy(
                    ln2Ts[:, 512 * fc + 128 * tb:512 * fc + 128 * tb + 128],
                    pbf(bank)[:, 0:128]),
                    reads=[pk(bank)], writes=[f"ln2_{fc}"])
        for nh in range(2):
            for c in range(NCH):
                gc = NCH * nh + c
                slb, slkeys = wgu[c % 2], WGUK[c % 2]
                kb.dma("sync", slb.rearrange("p (a b) -> p a b", a=2),
                       wguT_d[gc].rearrange("p (a b) -> p a b", a=2),
                       f"wgus{c % 2}", reads=[f"wguT_{gc}"], writes=slkeys)
                bg, bu = [(0, 1), (2, 3), (4, 5)][c % 3]

                def mmg(e, slb=slb, bg=bg):
                    for fc in range(FC):
                        ins = e.matmul(pf(bg), slb[:, 128 * fc:128 * fc + 128],
                                       ln2Ts[:, 512 * fc:512 * fc + 512],
                                       start=(fc == 0), stop=(fc == FC - 1))
                    return ins
                kb.op("tensor", mmg,
                      reads=slkeys + [f"ln2_{fc}" for fc in range(FC)],
                      writes=[pk(bg)])

                def mmu(e, slb=slb, bu=bu):
                    for fc in range(FC):
                        ins = e.matmul(pf(bu), slb[:, 2048 + 128 * fc:2048 + 128 * fc + 128],
                                       ln2Ts[:, 512 * fc:512 * fc + 512],
                                       start=(fc == 0), stop=(fc == FC - 1))
                    return ins
                kb.op("tensor", mmu,
                      reads=slkeys + [f"ln2_{fc}" for fc in range(FC)],
                      writes=[pk(bu)])
                kb.op("scalar", lambda e, bg=bg: e.copy(pex[0], pf(bg)),
                      reads=[pk(bg)], writes=["px0"])
                kb.op("vector", lambda e: e.tensor_tensor(
                    pex[1], pex[0], pex[0], OP.mult),
                    reads=["px0"], writes=["px1"])
                kb.op("vector", lambda e: e.tensor_scalar(
                    pex[2], pex[1], GC1, GC0, OP.mult, OP.add),
                    reads=["px1"], writes=["px2"])
                kb.op("vector", lambda e: e.tensor_tensor(
                    pex[3], pex[2], pex[0], OP.mult),
                    reads=["px2", "px0"], writes=["px3"])
                kb.op("scalar", lambda e: e.activation(
                    pex[1], pex[3], AF.Sigmoid),
                    reads=["px3"], writes=["px1"])
                kb.op("vector", lambda e: e.tensor_tensor(
                    pex[2], pex[1], pex[0], OP.mult),
                    reads=["px1", "px0"], writes=["px2"])
                kb.op("vector", lambda e, c=c, bu=bu: e.tensor_tensor(
                    interT[:, 512 * c:512 * c + 512], pex[2], pf(bu), OP.mult),
                    reads=["px2", pk(bu)], writes=[f"it_{c}"])
            for hh in range(2):
                for c in range(NCH):
                    gc = NCH * nh + c
                    dnb, dnkeys = wdn[c % 4], WDNK[c % 4]
                    kb.dma("sync", dnb, wdT_d[gc][:, 1024 * hh:1024 * hh + 1024],
                           f"wdns{c % 4}", reads=[f"wdT_{gc}"], writes=dnkeys)

                    def mmd(e, dnb=dnb, c=c):
                        for tb in range(4):
                            for hs in range(2):
                                ins = e.matmul(
                                    pf(2 * tb + hs),
                                    interT[:, 512 * c + 128 * tb:512 * c + 128 * tb + 128],
                                    dnb[:, 512 * hs:512 * hs + 512],
                                    start=(c == 0), stop=(c == NCH - 1))
                        return ins
                    kb.op("tensor", mmd,
                          reads=dnkeys + [f"it_{c}"],
                          writes=[pk(i) for i in range(8)])
                for tb in range(4):
                    kb.op("vector", lambda e, tb=tb: e.tensor_copy(
                        o1[:, 0:512], pf(2 * tb)),
                        reads=[pk(2 * tb)], writes=["o1"])
                    kb.op("vector", lambda e, tb=tb: e.tensor_copy(
                        o1[:, 512:1024], pf(2 * tb + 1)),
                        reads=[pk(2 * tb + 1), "o1"], writes=["o1"])
                    r0 = rowbase + 128 * tb
                    if nh == 0:
                        kb.dma("sync", x2p_d[r0:r0 + 128, 1024 * hh:1024 * hh + 1024],
                               o1[:, 0:1024], f"x2p_out{tb % 2}", reads=["o1"],
                               writes=[f"x2p_{tb}_{hh}_{tg}"])
                    else:
                        kb.dma("gpsimd", x2p_d[r0:r0 + 128, 1024 * hh:1024 * hh + 1024],
                               o1[:, 0:1024], f"x2pa{tb % 2}",
                               reads=["o1", f"x2p_{tb}_{hh}_{tg}"],
                               writes=[f"x2p_{tb}_{hh}_{tg}"],
                               accum_op=OP.add)
        for tb in range(TBS):
            r0 = rowbase + 128 * tb
            kb.dma("sync", xf[0][:], x2p_d[r0:r0 + 128, :], "x2p_in",
                   reads=[f"x2p_{tb}_{hh}_{tg}" for hh in range(2)], writes=["xf0"])
            kb.dma("sync", xf[1][:], x1_d[r0:r0 + 128, :], "x1_in2",
                   reads=[f"x1d_{r0 // 128}"], writes=["xf1"])
            ln_stats("xf0", xf[0][:], 16, 17, 18, 19)
            kb.op("vector", lambda e: e.scalar_tensor_tensor(
                stats[:, 20:21], stats[:, 16:17], 1.0, stats[:, 17:18],
                OP.mult, OP.mult),
                reads=["st16", "st17"], writes=["st20"])
            kb.op("vector", lambda e: e.tensor_scalar(
                o1[:], xf[1][:], stats[:, 20:21], 1.0, OP.subtract, OP.mult),
                reads=["xf1", "st20"], writes=["o1"])
            kb.op("vector", lambda e: e.scalar_tensor_tensor(
                xf[1][:], xf[0][:], stats[:, 17:18], o1[:], OP.mult, OP.add),
                reads=["xf0", "st17", "o1"], writes=["xf1"])
            seq = r0 // 2048
            srow = r0 % 2048
            kb.dma("sync", out[seq, srow:srow + 128, :], xf[1][:], "outp",
                   reads=["xf1"], writes=[f"out_{r0 // 128}"])

    for seq in range(BC):
        attn_phaseA(seq)
        attn_phaseB(seq)
    while dq_state["i"] < N_UNITS:
        dq_step(1, banks=(2, 3))
    kb.full_barrier()
    kb.new_phase()
    for tg in range(TG):
        ffn_tg(tg)
    kb.finalize(done_scr[:])

    with nc.Block() as block:
        @block.tensor
        def _(e):
            for f in kb.stream["tensor"]:
                f(e)

        @block.vector
        def _(e):
            for f in kb.stream["vector"]:
                f(e)

        @block.scalar
        def _(e):
            for f in kb.stream["scalar"]:
                f(e)

        @block.gpsimd
        def _(e):
            for f in kb.stream["gpsimd"]:
                f(e)

        @block.sync
        def _(e):
            for f in kb.stream["sync"]:
                f(e)

    return nc


_NC_CACHE = {}


def kernel(**inputs):
    x = np.ascontiguousarray(inputs["L0_attn_x"])
    if "nc" not in _NC_CACHE:
        _NC_CACHE["nc"] = build_nc()
    nc = _NC_CACHE["nc"]
    wkeys = ["L0_attn_Wq", "L0_attn_bq", "L0_attn_Wk", "L0_attn_bk",
             "L0_attn_Wv", "L0_attn_bv", "L0_attn_Wo", "L0_attn_bo",
             "L0_ffn_gate_W_packed", "L0_ffn_gate_scales",
             "L0_ffn_up_W_packed", "L0_ffn_up_scales",
             "L0_ffn_down_W_packed", "L0_ffn_down_scales"]
    shared = {k: np.ascontiguousarray(inputs[k]) for k in wkeys}
    in_maps = []
    for c in range(8):
        m = dict(shared)
        m["L0_attn_x"] = np.ascontiguousarray(x[2 * c:2 * c + 2])
        in_maps.append(m)
    res = run_bass_kernel_spmd(nc, in_maps, core_ids=list(range(8)))
    return np.concatenate([r["out"] for r in res.results], axis=0)


# version 09
# speedup vs baseline: 1.0987x; 1.0987x over previous
import numpy as np
import ml_dtypes
import concourse.bass as bass
import concourse.mybir as mybir
from concourse.bass_utils import run_bass_kernel_spmd

F32 = mybir.dt.float32
BF16 = mybir.dt.bfloat16
I32 = mybir.dt.int32
OP = mybir.AluOpType
AF = mybir.ActivationFunctionType
AX = mybir.AxisListType
bfdt = ml_dtypes.bfloat16

BC = 2
S = 2048
Fd = 2048
NHEAD = 32
NMLP = 16384
SCALE = float(np.sqrt(32.0))
FC = 16
JC = 16
NSUP = 4
TBS = 4
HP = 16
TCH = 16
TG = 8
NCH = 64
TOK = BC * S
GC0 = 1.5957691216057308
GC1 = 2.0 * 0.7978845608028654 * 0.044715

ENGINES = ("tensor", "vector", "scalar", "gpsimd", "sync")
DTSZ = {F32: 4, BF16: 2, I32: 4}


class KB:

    def __init__(self, nc):
        self.nc = nc
        self.stream = {e: [] for e in ENGINES}
        self.sem_h = {}
        self.sem_count = {}
        self.eng_waited = {e: {} for e in ENGINES}
        self.res = {}
        self.phase = 0
        self.esem = {}
        self.new_phase()

    def _sem(self, name):
        if name not in self.sem_h:
            self.sem_h[name] = self.nc.alloc_semaphore(name=name)
            self.sem_count[name] = 0
        return self.sem_h[name]

    def new_phase(self):
        self.phase += 1
        self.esem = {e: f"E{self.phase % 3}_{e}" for e in ENGINES}
        for e in ENGINES:
            self._sem(self.esem[e])

    def _wait(self, eng, semname, val):
        if val <= 0:
            return
        assert val <= self.sem_count[semname], (semname, val, self.sem_count[semname])
        w = self.eng_waited[eng]
        if w.get(semname, 0) >= val:
            return
        w[semname] = val
        h = self.sem_h[semname]
        self.stream[eng].append(lambda e, h=h, v=val: e.wait_ge(h, v))

    def _deps(self, eng, reads, writes):
        for k in reads:
            rec = self.res.get(k)
            if rec and rec["w"]:
                self._wait(eng, *rec["w"])
        for k in writes:
            rec = self.res.get(k)
            if rec:
                if rec["w"]:
                    self._wait(eng, *rec["w"])
                for (sn, v) in rec["r"]:
                    self._wait(eng, sn, v)

    def _record(self, token, reads, writes):
        for k in writes:
            self.res[k] = {"w": token, "r": []}
        for k in reads:
            self.res.setdefault(k, {"w": None, "r": []})["r"].append(token)

    def op(self, eng, fn, reads=(), writes=()):
        self._deps(eng, reads, writes)
        semname = self.esem[eng]
        self.sem_count[semname] += 1
        h = self.sem_h[semname]

        def emit(e, fn=fn, h=h):
            fn(e).then_inc(h, 1)
        self.stream[eng].append(emit)
        token = (semname, self.sem_count[semname])
        self._record(token, reads, writes)
        return token

    def dma(self, queue, out, in_, sname, reads=(), writes=(), **kw):
        self._deps(queue, reads, writes)
        semname = "D_" + sname
        self._sem(semname)
        self._wait(queue, semname, self.sem_count[semname])
        self.sem_count[semname] += 16
        h = self.sem_h[semname]

        def emit(e, out=out, in_=in_, h=h, kw=kw):
            e.dma_start(out=out, in_=in_, **kw).then_inc(h, 16)
        self.stream[queue].append(emit)
        token = (semname, self.sem_count[semname])
        self._record(token, reads, writes)
        return token

    def full_barrier(self):
        counts = [(n, c) for n, c in self.sem_count.items() if c > 0]
        for e in ENGINES:
            for n, c in counts:
                self._wait(e, n, c)
        self.res = {}

    def finalize(self, done_scratch):
        done = self._sem("DONE")
        for e in ("vector", "scalar", "gpsimd"):
            def emit(en, d=done, scr=done_scratch):
                en.memzero(scr).then_inc(d, 1)
            self.stream[e].append(emit)
        self.sem_count["DONE"] = 3
        for name, cnt in list(self.sem_count.items()):
            if cnt > 0 and name != "DONE":
                self._wait("sync", name, cnt)
        self.stream["sync"].append(lambda e, h=done: e.wait_ge(h, 3))
        handles = list(self.sem_h.values())

        def clear(e, hs=handles):
            for hh in hs:
                e.sem_clear(hh)
        self.stream["sync"].append(clear)


def gu_row_start(c):
    j = c % 4
    return 512 * (c // 4) + (j % 2) * 256 + (j // 2)


def build_nc():
    nc = bass.Bass("TRN2", target_bir_lowering=False)

    x_in = nc.dram_tensor("L0_attn_x", [BC, S, Fd], F32, kind="ExternalInput").ap()
    Wq = nc.dram_tensor("L0_attn_Wq", [Fd, Fd], F32, kind="ExternalInput").ap()
    bq = nc.dram_tensor("L0_attn_bq", [Fd], F32, kind="ExternalInput").ap()
    Wk = nc.dram_tensor("L0_attn_Wk", [Fd, Fd], F32, kind="ExternalInput").ap()
    bk = nc.dram_tensor("L0_attn_bk", [Fd], F32, kind="ExternalInput").ap()
    Wv = nc.dram_tensor("L0_attn_Wv", [Fd, Fd], F32, kind="ExternalInput").ap()
    bv = nc.dram_tensor("L0_attn_bv", [Fd], F32, kind="ExternalInput").ap()
    Wo = nc.dram_tensor("L0_attn_Wo", [Fd, Fd], F32, kind="ExternalInput").ap()
    bo = nc.dram_tensor("L0_attn_bo", [Fd], F32, kind="ExternalInput").ap()
    gpk = nc.dram_tensor("L0_ffn_gate_W_packed", [NMLP, Fd // 2], I32, kind="ExternalInput").ap()
    gsc = nc.dram_tensor("L0_ffn_gate_scales", [NMLP, Fd // 32], F32, kind="ExternalInput").ap()
    upk = nc.dram_tensor("L0_ffn_up_W_packed", [NMLP, Fd // 2], I32, kind="ExternalInput").ap()
    usc = nc.dram_tensor("L0_ffn_up_scales", [NMLP, Fd // 32], F32, kind="ExternalInput").ap()
    dpk = nc.dram_tensor("L0_ffn_down_W_packed", [Fd, NMLP // 2], I32, kind="ExternalInput").ap()
    dsc = nc.dram_tensor("L0_ffn_down_scales", [Fd, NMLP // 32], F32, kind="ExternalInput").ap()
    out = nc.dram_tensor("out", [BC, S, Fd], F32, kind="ExternalOutput").ap()

    lnT_d = nc.dram_tensor("lnT_d", [Fd, S], BF16, kind="Internal").ap()
    kT_d = nc.dram_tensor("kT_d", [JC, 128, S], BF16, kind="Internal").ap()
    x1_d = nc.dram_tensor("x1_d", [TOK, Fd], F32, kind="Internal").ap()
    x2p_d = nc.dram_tensor("x2p_d", [TOK, Fd], F32, kind="Internal").ap()
    wguT_d = nc.dram_tensor("wguT_d", [128, 128, 2 * Fd], BF16, kind="Internal").ap()
    wqkvo_d = nc.dram_tensor("wqkvo_d", [4, Fd, Fd], BF16, kind="Internal").ap()
    wqkT_d = nc.dram_tensor("wqkT_d", [2, JC, 128, Fd], BF16, kind="Internal").ap()
    wdT_d = nc.dram_tensor("wdT_d", [128, 128, Fd], BF16, kind="Internal").ap()

    ident = nc.inline_tensor(np.eye(128, dtype=bfdt), "ident")
    ones_row = nc.inline_tensor(np.ones((1, 128), dtype=bfdt), "ones_row")
    E_np = np.zeros((JC, 128, NHEAD), dtype=bfdt)
    for jc in range(JC):
        E_np[jc, 0:64, 2 * jc] = 1
        E_np[jc, 64:128, 2 * jc + 1] = 1
    E_d = nc.inline_tensor(E_np.reshape(JC * 128, NHEAD), "E_ind")

    def sb(name, shape, dtype):
        return nc.alloc_sbuf_tensor(name, shape, dtype)

    ident_s = sb("ident_s", [128, 128], BF16)
    ones_s = sb("ones_s", [1, 128], BF16)
    E_s = sb("E_s", [128, JC * NHEAD], BF16)
    bq_t = sb("bq_t", [128, JC], F32)
    bk_t = sb("bk_t", [128, JC], F32)

    Brow = sb("Brow", [1, 32], BF16)
    Bcol = sb("Bcol", [128, 1], BF16)
    eps_c = sb("eps_c", [128, 1], F32)
    stats = sb("stats", [128, 24], F32)
    negBb = sb("negBb", [128, NHEAD], F32)
    ZtR = sb("ZtR", [128, 128], F32)
    mkq = sb("mkq", [128, 8], F32)
    done_scr = sb("done_scr", [1, 1], F32)
    DQC = 512
    stage = [sb(f"stage{i}", [128, 2048], BF16) for i in range(4)]
    xf = [sb(f"xf{i}", [128, Fd], F32) for i in range(2)]
    o1 = sb("o1", [128, Fd], F32)
    p1 = sb("p1", [128, 8192], BF16)
    wp = sb("wp", [128, 8192], BF16)

    p1a = p1[:, 0:2048]
    p1b = p1[:, 2048:4096]
    pex = [p1[:, 4096 + 512 * i:4096 + 512 * i + 512] for i in range(8)]
    PXK = [f"px{i}" for i in range(8)]
    p1d = p1[:, 6144:8192]
    P1DK = ["px4", "px5", "px6", "px7"]
    bvrow = p1[0:1, 4096:6144]
    borow = p1[0:1, 4096:6144]
    BRK = ["px0", "px1", "px2", "px3"]

    wslot = [wp[:, 2048 * i:2048 * i + 2048] for i in range(4)]

    def qkeys(i):
        return [f"Q{i}{c}" for c in "abcd"]

    wvbuf = [wp[:, 4096:4608], wp[:, 6144:6656], wp[:, 4608:5120], wp[:, 6656:7168]]
    WVK = [["Q2a"], ["Q3a"], ["Q2b"], ["Q3b"]]
    wgu = [wp[:, 0:4096], wp[:, 4096:8192]]
    WGUK = [qkeys(0) + qkeys(1), qkeys(2) + qkeys(3)]
    wdn = [wp[:, 0:1024], wp[:, 2048:3072], wp[:, 4096:5120], wp[:, 6144:7168]]
    WDNK = [["Q0a", "Q0b"], ["Q1a", "Q1b"], ["Q2a", "Q2b"], ["Q3a", "Q3b"]]

    ZONE = nc.sbuf_base + 32
    ZEND = nc.sbuf_top

    def at(name, shape, dtype, off):
        h = nc.alloc_sbuf_tensor_at(name, shape, dtype, offset=off)
        return h, off + int(np.prod(shape[1:])) * DTSZ[dtype]

    az = ZONE
    vb, az = at("vb", [128, 16 * NHEAD * 65], BF16, az)
    qTb, az = at("qTb", [128, JC * 512], BF16, az)
    fUT, az = at("fUT", [128, HP * 512], BF16, az)
    lnTs2, az = at("lnTs2", [128, FC * 512], BF16, az)
    kTs_t, az = at("kTs", [128, 2 * S], BF16, az)
    fz = ZONE
    interT, fz = at("interT", [128, NCH * 512], BF16, fz)
    ln2Ts, fz = at("ln2Ts", [128, FC * 512], BF16, fz)
    assert az >= fz, "dequant buffers must start above the FFN zone"
    pkb = []
    for i in range(2):
        h, az = at(f"pkb{i}", [128, 512], I32, az)
        pkb.append(h)
    dqt = []
    for i in range(2):
        h, az = at(f"dqt{i}", [128, DQC], I32, az)
        dqt.append(h)
    wnat = []
    for i in range(3):
        h, az = at(f"wnat{i}", [128, 512], BF16, az)
        wnat.append(h)
    scl = []
    for i in range(2):
        h, az = at(f"scl{i}", [128, 64], F32, az)
        scl.append(h)
    dscl, az = at("dscl", [128, 64], F32, az)
    assert az <= ZEND, f"attention zone overflow: {az} > {ZEND}"

    lnTs = fUT
    kTs = [kTs_t[:, 0:2048], kTs_t[:, 2048:4096]]

    kb = KB(nc)

    psg = [nc.psum_tensor(f"pb{i}", [128, 512], F32) for i in range(8)]
    pst = [g.__enter__() for g in psg]

    def pf(i, n=512):
        return pst[i][:, 0:n]

    def pbf(i):
        return pst[i][:].bitcast(BF16)

    def pk(i):
        return f"pb{i}"

    kb.dma("sync", bass.AP(ident_s, 0, [[128, 128], [1, 128]]), ident.ap(), "pre",
           writes=["ident"])
    kb.dma("sync", bass.AP(ones_s, 0, [[128, 1], [1, 128]]), ones_row.ap(), "pre",
           writes=["ones"])
    kb.dma("sync", E_s[:].rearrange("p (jc h) -> p jc h", jc=JC),
           E_d.ap().rearrange("(jc p) h -> p jc h", p=128), "pre", writes=["E"])
    for jc in range(JC):
        kb.dma("sync", bq_t[:, jc:jc + 1],
               bq[128 * jc:128 * jc + 128].rearrange("(p one) -> p one", one=1),
               "pre", writes=["bq"])
        kb.dma("sync", bk_t[:, jc:jc + 1],
               bk[128 * jc:128 * jc + 128].rearrange("(p one) -> p one", one=1),
               "pre", writes=["bk"])
    kb.op("gpsimd", lambda e: e.memset(eps_c[:], 1e-5), writes=["eps"])
    ones_view = bass.AP(vb, 64, [[16 * NHEAD * 65, 128], [NHEAD * 65, 16], [65, NHEAD]])
    kb.op("gpsimd", lambda e: e.memset(ones_view, 1.0), writes=["vbones"])
    for m, Wsrc in ((1, Wk), (2, Wv), (0, Wq), (3, Wo)):
        role = {1: 0, 0: 1}.get(m)
        for fc in range(FC):
            qslot = fc % 4
            buf = wslot[qslot]
            kb.dma("gpsimd", buf, Wsrc[128 * fc:128 * fc + 128, :],
                   f"pcin{qslot}", writes=qkeys(qslot))
            if role is not None:
                kb.dma("sync",
                       wqkT_d[role][:, :, 128 * fc:128 * fc + 128].rearrange(
                           "jc p j -> p jc j"),
                       buf.rearrange("p (jc j) -> p jc j", jc=JC),
                       f"pcout{qslot}", reads=qkeys(qslot),
                       writes=[f"wb_{role}_{fc}"])
            else:
                kb.dma("sync", wqkvo_d[m, 128 * fc:128 * fc + 128, :], buf,
                       f"pcout{qslot}", reads=qkeys(qslot), writes=[f"wc_{m}_{fc}"])

    P126 = float(2.0 ** 126)
    PKK = [[f"pkb{i}a", f"pkb{i}b"] for i in range(2)]
    DQTK = [[f"dqt{i}a", f"dqt{i}b"] for i in range(2)]

    def dq_nibble(pap, pkkeys, sckey, sbc, dst_ap, dstkey, lohi, t, tkeys):
        sh = 28 if lohi == 0 else 24
        n = pap.shape[1]
        kb.op("vector", lambda e: e.tensor_scalar(
            t[:, 0:n], pap, sh, 6, OP.logical_shift_left, OP.arith_shift_right),
            reads=pkkeys, writes=tkeys)
        kb.op("vector", lambda e: e.tensor_scalar(
            t[:, 0:n], t[:, 0:n], 0x81C00000 - 2 ** 32, 0, OP.bitwise_and, OP.bitwise_or),
            reads=tkeys, writes=tkeys)
        kb.op("gpsimd", lambda e: e.tensor_tensor(
            dst_ap.rearrange("p (b i) -> p b i", i=16),
            t[:, 0:n].bitcast(F32).rearrange("p (b i) -> p b i", i=16),
            sbc, OP.mult),
            reads=tkeys + [sckey], writes=[dstkey])

    dq_state = {"banks": (0, 1)}

    def emit_gu_slab_dve(c):
        rs = gu_row_start(c)
        for half, (pk_src, sc_src) in enumerate(((gpk, gsc), (upk, usc))):
            wns = stage[2 + half]
            wnkey = f"stage{2 + half}"
            sbuf_ = scl[half]
            sckey = f"dqsc{half}"
            kb.dma("gpsimd", sbuf_[:], sc_src[rs:rs + 255:2, :], f"dqscs{half}",
                   writes=[sckey])
            kb.op("vector", lambda e, s=sbuf_: e.tensor_scalar(
                s[:], s[:], P126, 1.0, OP.mult, OP.mult),
                reads=[sckey], writes=[sckey])
            for pl in range(2):
                pbuf = pkb[pl]
                kb.dma("gpsimd", pbuf[:],
                       pk_src[rs:rs + 255:2, 512 * pl:512 * pl + 512],
                       f"dqpks{pl}", writes=PKK[pl])
                pap = pbuf[:]
                sbc = sbuf_[:, 32 * pl:32 * pl + 32].unsqueeze(2).broadcast_to(
                    [128, 32, 16])
                for lohi in range(2):
                    dq_nibble(pap, PKK[pl], sckey, sbc,
                              wns[:, 1024 * lohi + 512 * pl:
                                  1024 * lohi + 512 * pl + 512], wnkey, lohi,
                              dqt[lohi], DQTK[lohi])

    def emit_gu_slab_pe(c):
        for half in range(2):
            wns = stage[2 + half]
            wnkey = f"stage{2 + half}"
            stg = stage[half]
            stgkey = f"stage{half}"
            bank = dq_state["banks"][half]
            for grp in range(4):
                for sub in range(4):
                    fcid = 4 * grp + sub
                    kb.op("tensor", lambda e, fcid=fcid, sub=sub, wns=wns, bank=bank:
                          e.transpose(
                              pbf(bank)[:, 128 * sub:128 * sub + 128],
                              wns[:, 128 * fcid:128 * fcid + 128], ident_s[:]),
                          reads=[wnkey, "ident"], writes=[pk(bank)])
                kb.op("vector", lambda e, grp=grp, stg=stg, bank=bank: e.tensor_copy(
                    stg[:, 512 * grp:512 * grp + 512], pbf(bank)[:, 0:512]),
                    reads=[pk(bank)], writes=[stgkey])
            kb.dma("gpsimd", wguT_d[c, :, 2048 * half:2048 * half + 2048], stg[:],
                   "wguT_out", reads=[stgkey], writes=[f"wguT_{c}"])

    N_GU = 128
    gu_state = {"dve": 0, "pe": 0}

    def gu_step(n=1, banks=None):
        if banks is not None:
            dq_state["banks"] = banks
        for _ in range(n):
            if gu_state["dve"] > gu_state["pe"]:
                emit_gu_slab_pe(gu_state["pe"])
                gu_state["pe"] += 1
            if gu_state["dve"] < N_GU:
                emit_gu_slab_dve(gu_state["dve"])
                gu_state["dve"] += 1

    def gu_flush(banks=None):
        if banks is not None:
            dq_state["banks"] = banks
        while gu_state["pe"] < gu_state["dve"]:
            emit_gu_slab_pe(gu_state["pe"])
            gu_state["pe"] += 1

    down_state = {"g": 0, "pend": []}

    def down_micro(n=1, banks=None):
        if banks is not None:
            dq_state["banks"] = banks
        for _ in range(n):
            g = down_state["g"]
            if g >= 512:
                return
            u, hc = g // 16, g % 16
            s4 = g % 4
            pkv = pkb[s4 // 2][:, 256 * (s4 % 2):256 * (s4 % 2) + 256]
            pkkeys = [PKK[s4 // 2][s4 % 2]]
            kb.dma("gpsimd", pkv,
                   dpk[128 * hc:128 * hc + 128, 256 * u:256 * u + 256],
                   f"ddpk{s4}", writes=pkkeys)
            scv = dscl[:, 16 * s4:16 * s4 + 16]
            sckey = f"dscl{s4}"
            kb.dma("gpsimd", scv,
                   dsc[128 * hc:128 * hc + 128, 16 * u:16 * u + 16],
                   f"ddsc{s4}", writes=[sckey])
            kb.op("vector", lambda e, scv=scv: e.tensor_scalar(
                scv, scv, P126, 1.0, OP.mult, OP.mult),
                reads=[sckey], writes=[sckey])
            wn = wnat[g % 3]
            wnk = f"dqwn{g % 3}"
            sbc = scv.unsqueeze(2).broadcast_to([128, 16, 16])
            for lohi in range(2):
                tv = dqt[lohi][:, 256 * (g % 2):256 * (g % 2) + 256]
                dq_nibble(pkv, pkkeys, sckey, sbc,
                          wn[:, 256 * lohi:256 * lohi + 256], wnk + str(lohi),
                          lohi, tv, [DQTK[lohi][g % 2]])
            banks_now = dq_state["banks"]

            def pe_part(u=u, hc=hc, wn=wn, wnk=wnk, banks=banks_now):
                for lohi in range(2):
                    for sub in range(2):
                        sl = 2 * lohi + sub
                        bank = banks[lohi]
                        kb.op("tensor", lambda e, wn=wn, lohi=lohi, sub=sub, bank=bank:
                              e.transpose(
                                  pbf(bank)[:, 0:128],
                                  wn[:, 256 * lohi + 128 * sub:
                                     256 * lohi + 128 * sub + 128],
                                  ident_s[:]),
                              reads=[wnk + str(lohi), "ident"], writes=[pk(bank)])
                        kb.op("vector", lambda e, hc=hc, sl=sl, bank=bank: e.tensor_copy(
                            stage[sl][:, 128 * hc:128 * hc + 128], pbf(bank)[:, 0:128]),
                            reads=[pk(bank)], writes=[f"stage{sl}"])
                if hc == 15:
                    for sl in range(4):
                        kb.dma("gpsimd", wdT_d[4 * u + sl, :, :], stage[sl][:],
                               "wdT_out", reads=[f"stage{sl}"],
                               writes=[f"wdT_{4 * u + sl}"])
            down_state["pend"].append(pe_part)
            if len(down_state["pend"]) > 2:
                down_state["pend"].pop(0)()
            down_state["g"] += 1

    def down_flush():
        for f in down_state["pend"]:
            f()
        down_state["pend"] = []

    def ln_from_sums(csum, csumsq, cs1, cs2):
        t1 = stats[:, cs1:cs1 + 1]
        t2 = stats[:, cs2:cs2 + 1]
        kb.op("vector", lambda e: e.tensor_scalar(
            stats[:, csum:csum + 1], stats[:, csum:csum + 1], 1.0 / Fd, 1.0,
            OP.mult, OP.mult), reads=[f"st{csum}"], writes=[f"st{csum}"])
        kb.op("vector", lambda e: e.scalar_tensor_tensor(
            t1, stats[:, csum:csum + 1], 1.0, stats[:, csum:csum + 1],
            OP.mult, OP.mult), reads=[f"st{csum}"], writes=[f"st{cs1}"])
        kb.op("vector", lambda e: e.tensor_scalar(
            stats[:, csumsq:csumsq + 1], stats[:, csumsq:csumsq + 1], 1.0 / Fd, 1e-5,
            OP.mult, OP.add), reads=[f"st{csumsq}"], writes=[f"st{csumsq}"])
        kb.op("vector", lambda e: e.tensor_tensor(
            stats[:, csumsq:csumsq + 1], stats[:, csumsq:csumsq + 1], t1,
            OP.subtract), reads=[f"st{csumsq}", f"st{cs1}"], writes=[f"st{csumsq}"])
        kb.op("vector", lambda e: e.tensor_scalar(
            t1.bitcast(I32), stats[:, csumsq:csumsq + 1].bitcast(I32), 1, 0,
            OP.logical_shift_right, OP.bitwise_or),
            reads=[f"st{csumsq}"], writes=[f"st{cs1}"])
        kb.op("vector", lambda e: e.tensor_scalar(
            t1.bitcast(I32), t1.bitcast(I32), -1, 0x5f3759df, OP.mult, OP.add),
            reads=[f"st{cs1}"], writes=[f"st{cs1}"])
        for _ in range(2):
            kb.op("vector", lambda e: e.scalar_tensor_tensor(
                t2, t1, 1.0, t1, OP.mult, OP.mult),
                reads=[f"st{cs1}"], writes=[f"st{cs2}"])
            kb.op("vector", lambda e: e.tensor_tensor(
                t2, t2, stats[:, csumsq:csumsq + 1], OP.mult),
                reads=[f"st{cs2}", f"st{csumsq}"], writes=[f"st{cs2}"])
            kb.op("vector", lambda e: e.tensor_scalar(
                t2, t2, -0.5, 1.5, OP.mult, OP.add),
                reads=[f"st{cs2}"], writes=[f"st{cs2}"])
            kb.op("vector", lambda e: e.tensor_tensor(
                t1, t1, t2, OP.mult),
                reads=[f"st{cs1}", f"st{cs2}"], writes=[f"st{cs1}"])
        kb.op("vector", lambda e: e.tensor_copy(
            stats[:, csumsq:csumsq + 1], t1),
            reads=[f"st{cs1}"], writes=[f"st{csumsq}"])

    def ln_stats(xkey, xap, csum, csumsq, cs1, cs2, junk=None, junkkey=("o1",)):
        if isinstance(xkey, str):
            xkey = [xkey]
        jap = o1[:] if junk is None else junk
        junkkey = list(junkkey)
        kb.op("vector", lambda e: e.scalar_tensor_tensor(
            jap, xap, 1.0, xap, OP.mult, OP.max,
            accum_out=stats[:, csum:csum + 1]),
            reads=xkey, writes=junkkey + [f"st{csum}"])
        kb.op("vector", lambda e: e.scalar_tensor_tensor(
            jap, xap, 1.0, xap, OP.mult, OP.mult,
            accum_out=stats[:, csumsq:csumsq + 1]),
            reads=xkey + junkkey, writes=junkkey + [f"st{csumsq}"])
        ln_from_sums(csum, csumsq, cs1, cs2)

    def ln_tile_transposed(seq, trow, dst, keyfmt, colbase, bufi):
        xkey = f"xf{bufi}"
        kb.dma("scalar", xf[bufi][:], x_in[seq, trow:trow + 128, :], f"xfs{bufi}",
               writes=[xkey])
        cb = 4 * bufi
        jk = [("o1",), ("p1b",)][bufi]
        jap = [o1[:], p1b][bufi]
        ln_stats(xkey, xf[bufi][:], cb, cb + 1, cb + 2, cb + 3, jap, jk)
        kb.op("vector", lambda e: e.tensor_scalar(
            p1a, xf[bufi][:], stats[:, cb:cb + 1], stats[:, cb + 1:cb + 2],
            OP.subtract, OP.mult),
            reads=[xkey, f"st{cb}", f"st{cb + 1}"], writes=["p1a"])
        for fc in range(FC):
            bank = 4 + fc % 2
            kb.op("tensor", lambda e, fc=fc, bank=bank: e.transpose(
                pbf(bank)[:, 0:128], p1a[:, 128 * fc:128 * fc + 128], ident_s[:]),
                reads=["p1a", "ident"], writes=[pk(bank)])
            kb.op("vector", lambda e, fc=fc, bank=bank: e.tensor_copy(
                dst[:, 512 * fc + colbase:512 * fc + colbase + 128],
                pbf(bank)[:, 0:128]),
                reads=[pk(bank)], writes=[keyfmt.format(i=fc)])

    def attn_phaseA(seq):
        for sup in range(NSUP):
            kb.new_phase()
            kb.dma("gpsimd", bvrow, bv.unsqueeze(0), "bvrow", writes=BRK)
            for tb in range(TBS):
                ln_tile_transposed(seq, 512 * sup + 128 * tb, lnTs, "fu_{i}",
                                   128 * tb, tb % 2)
            for fc in range(FC):
                kb.dma("sync", lnT_d[128 * fc:128 * fc + 128, 512 * sup:512 * sup + 512],
                       lnTs[:, 512 * fc:512 * fc + 512], "lnT_out",
                       reads=[f"fu_{fc}"], writes=[f"lnTd_{fc}_{sup}"])
            for jc in range(JC):
                sl, slkeys = wslot[jc % 2], qkeys(jc % 2)
                kb.dma("sync", sl, wqkT_d[0, jc], f"wsls{jc % 2}",
                       reads=[f"wb_0_{fc}" for fc in range(FC)], writes=slkeys)
                bank = 6 + jc % 2

                def mm(e, sl=sl, bank=bank):
                    for fc in range(FC):
                        ins = e.matmul(pf(bank), sl[:, 128 * fc:128 * fc + 128],
                                       lnTs[:, 512 * fc:512 * fc + 512],
                                       start=(fc == 0), stop=(fc == FC - 1))
                    return ins
                kb.op("tensor", mm,
                      reads=slkeys + [f"fu_{fc}" for fc in range(FC)],
                      writes=[pk(bank)])
                kt, ktkey = kTs[jc % 2], f"kTs{jc % 2}"
                kb.op("vector", lambda e, bank=bank, jc=jc, kt=kt: e.tensor_scalar(
                    kt[:, 0:512], pf(bank), bk_t[:, jc:jc + 1], 1.0, OP.add, OP.mult),
                    reads=[pk(bank), "bk"], writes=[ktkey])
                kb.dma("scalar", kT_d[jc, :, 512 * sup:512 * sup + 512], kt[:, 0:512],
                       "kT_out", reads=[ktkey], writes=[f"kTd_{jc}_{sup}"])
                down_micro(2, banks=(4, 5))
            for hs in range(4):
                for fc in range(FC):
                    ch, chkeys = wvbuf[fc % 4], WVK[fc % 4]
                    kb.dma("sync" if fc % 2 == 0 else "gpsimd", ch,
                           wqkvo_d[2, 128 * fc:128 * fc + 128, 512 * hs:512 * hs + 512],
                           f"wvs{fc % 4}", reads=[f"wc_2_{fc}"], writes=chkeys)

                    def mmv(e, ch=ch, fc=fc):
                        for tb2 in range(4):
                            ins = e.matmul(
                                pf(tb2),
                                lnTs[:, 512 * fc + 128 * tb2:512 * fc + 128 * tb2 + 128],
                                ch, start=(fc == 0), stop=False)
                        return ins
                    kb.op("tensor", mmv,
                          reads=chkeys + [f"fu_{fc2}" for fc2 in range(FC)],
                          writes=[pk(0), pk(1), pk(2), pk(3)])
                    if fc % 2 == 0:
                        down_micro(1, banks=(4, 5))

                def mmvb(e, hs=hs):
                    for tb2 in range(4):
                        ins = e.matmul(pf(tb2), ones_s[0:1, :],
                                       bvrow[0:1, 512 * hs:512 * hs + 512],
                                       start=False, stop=True)
                    return ins
                kb.op("tensor", mmvb, reads=["ones", f"px{hs}"],
                      writes=[pk(0), pk(1), pk(2), pk(3)])
                for tb2 in range(4):
                    tbg = 4 * sup + tb2
                    dst = bass.AP(vb, 2080 * tbg + 65 * 8 * hs,
                                  [[16 * NHEAD * 65, 128], [65, 8], [1, 64]])
                    kb.op("vector", lambda e, tb2=tb2, dst=dst: e.tensor_copy(
                        dst, pf(tb2).rearrange("p (h d) -> p h d", d=64)),
                        reads=[pk(tb2)], writes=[f"vb_{tbg}_{hs}"])
        down_flush()
        assert down_state["g"] % 16 == 0, down_state["g"]
        kb.new_phase()
        for ts4 in range(4):
            for jc in range(JC):
                kt, ktkey = kTs[jc % 2], f"kTs{jc % 2}"
                kb.dma("sync", kt[:, 0:512], kT_d[jc, :, 512 * ts4:512 * ts4 + 512],
                       f"kTss{jc % 2}", reads=[f"kTd_{jc}_{ts4}"], writes=[ktkey])
                kb.op("vector", lambda e, kt=kt: e.tensor_tensor(
                    p1a[:, 0:512], kt[:, 0:512], kt[:, 0:512], OP.mult),
                    reads=[ktkey], writes=["p1a"])
                kb.op("tensor", lambda e, jc=jc: e.matmul(
                    pf(7)[0:32, :], E_s[:, 32 * jc:32 * jc + 32], p1a[:, 0:512],
                    start=(jc == 0), stop=(jc == JC - 1)),
                    reads=["p1a", "E"], writes=[pk(7)])
            kb.op("vector", lambda e: e.tensor_reduce(
                stats[0:32, 16:17], pf(7)[0:32, :], AX.X, OP.max),
                reads=[pk(7)], writes=["st16"])
            if ts4 == 0:
                kb.op("vector", lambda e: e.tensor_copy(
                    mkq[0:32, 0:1], stats[0:32, 16:17]),
                    reads=["st16"], writes=["maxk2"])
            else:
                kb.op("vector", lambda e: e.tensor_tensor(
                    mkq[0:32, 0:1], mkq[0:32, 0:1], stats[0:32, 16:17], OP.max),
                    reads=["st16", "maxk2"], writes=["maxk2"])

    def attn_phaseB(seq):
        for sup in range(NSUP):
            kb.new_phase()
            for fc in range(FC):
                kb.dma("sync", lnTs2[:, 512 * fc:512 * fc + 512],
                       lnT_d[128 * fc:128 * fc + 128, 512 * sup:512 * sup + 512],
                       f"lnT_in{fc % 2}", reads=[f"lnTd_{fc}_{sup}"],
                       writes=[f"l2_{fc}"])
            for hp0 in range(2):
                kb.dma("sync", kTs[hp0], kT_d[hp0, :, :], f"kTss{hp0}",
                       reads=[f"kTd_{hp0}_{s2}" for s2 in range(4)],
                       writes=[f"kTs{hp0}"])
            for jc in range(JC):
                sl, slkeys = wslot[jc % 2], qkeys(jc % 2)
                kb.dma("sync", sl, wqkT_d[1, jc], f"wsls{jc % 2}",
                       reads=[f"wb_1_{fc}" for fc in range(FC)], writes=slkeys)
                bank = 6 + jc % 2

                def mm(e, sl=sl, bank=bank):
                    for fc in range(FC):
                        ins = e.matmul(pf(bank), sl[:, 128 * fc:128 * fc + 128],
                                       lnTs2[:, 512 * fc:512 * fc + 512],
                                       start=(fc == 0), stop=(fc == FC - 1))
                    return ins
                kb.op("tensor", mm,
                      reads=slkeys + [f"l2_{fc}" for fc in range(FC)],
                      writes=[pk(bank)])
                kb.op("vector", lambda e, bank=bank, jc=jc: e.tensor_scalar(
                    qTb[:, 512 * jc:512 * jc + 512], pf(bank),
                    bq_t[:, jc:jc + 1], 1.0, OP.add, OP.mult),
                    reads=[pk(bank), "bq"], writes=[f"qTb_{jc}"])
            for jc in range(JC):
                kb.op("vector", lambda e, jc=jc: e.tensor_tensor(
                    p1a[:, 0:512], qTb[:, 512 * jc:512 * jc + 512],
                    qTb[:, 512 * jc:512 * jc + 512], OP.mult),
                    reads=[f"qTb_{jc}"], writes=["p1a"])
                kb.op("tensor", lambda e, jc=jc: e.matmul(
                    pf(7)[0:32, :], E_s[:, 32 * jc:32 * jc + 32], p1a[:, 0:512],
                    start=(jc == 0), stop=(jc == JC - 1)),
                    reads=["p1a", "E"], writes=[pk(7)])
            kb.op("vector", lambda e: e.tensor_reduce(
                mkq[0:32, 1:2], pf(7)[0:32, :], AX.X, OP.max),
                reads=[pk(7)], writes=["maxq2"])
            kb.op("vector", lambda e: e.tensor_tensor(
                mkq[0:32, 2:3], mkq[0:32, 0:1], mkq[0:32, 1:2], OP.mult),
                reads=["maxk2", "maxq2"], writes=["B2"])
            kb.op("scalar", lambda e: e.activation(
                mkq[0:32, 3:4], mkq[0:32, 2:3], AF.Ln),
                reads=["B2"], writes=["B3"])
            kb.op("scalar", lambda e: e.activation(
                mkq[0:32, 4:5], mkq[0:32, 3:4], AF.Exp, scale=0.5),
                reads=["B3"], writes=["B4"])
            kb.op("vector", lambda e: e.tensor_scalar(
                mkq[0:32, 5:6], mkq[0:32, 4:5], -1.0 / SCALE, 1.0, OP.mult, OP.mult),
                reads=["B4"], writes=["B5"])
            kb.op("vector", lambda e: e.tensor_copy(
                Bcol[0:32, :], mkq[0:32, 5:6]),
                reads=["B5"], writes=["Bcol"])
            kb.op("tensor", lambda e: e.transpose(
                pbf(6)[0:1, 0:32], Bcol[0:32, :], ident_s[0:32, 0:32]),
                reads=["Bcol", "ident"], writes=[pk(6)])
            kb.op("vector", lambda e: e.tensor_copy(
                Brow[0:1, :], pbf(6)[0:1, 0:32]),
                reads=[pk(6)], writes=["Brow"])
            kb.op("tensor", lambda e: e.matmul(
                pf(7)[:, 0:32], ones_s[0:1, :], Brow[0:1, :], start=True, stop=True),
                reads=["Brow", "ones"], writes=[pk(7)])
            kb.op("vector", lambda e: e.tensor_copy(
                negBb[:, :], pf(7)[:, 0:32]),
                reads=[pk(7)], writes=["negBb"])

            for hp in range(HP):
                kt, ktkey = kTs[hp % 2], f"kTs{hp % 2}"
                pend_av = []

                def mk_av(tc, w0=None, w1=None, k0=None, k1=None, hp=hp):
                    def go():
                        vt0 = vb[:, 2080 * tc + 65 * (2 * hp):
                                 2080 * tc + 65 * (2 * hp) + 65]
                        vt1 = vb[:, 2080 * tc + 65 * (2 * hp + 1):
                                 2080 * tc + 65 * (2 * hp + 1) + 65]
                        vkeys = [f"vb_{tc}_{hs}" for hs in range(4)] + ["vbones"]
                        kb.op("tensor", lambda e: e.matmul(
                            pf(4)[0:65, :], vt0, w0, start=(tc == 0),
                            stop=(tc == TCH - 1)),
                            reads=[k0] + vkeys, writes=[pk(4)])
                        kb.op("tensor", lambda e: e.matmul(
                            pf(5)[0:65, :], vt1, w1, start=(tc == 0),
                            stop=(tc == TCH - 1)),
                            reads=[k1] + vkeys, writes=[pk(5)])
                    return go

                for tc in range(TCH):
                    b0, b1 = (0, 1) if tc % 2 == 0 else (2, 3)
                    w0, w1 = pex[(2 * tc) % 8], pex[(2 * tc + 1) % 8]
                    k0, k1 = PXK[(2 * tc) % 8], PXK[(2 * tc + 1) % 8]
                    kb.op("tensor", lambda e, kt=kt, tc=tc, b0=b0, hp=hp: e.matmul(
                        pf(b0), kt[0:64, 128 * tc:128 * tc + 128],
                        qTb[0:64, 512 * hp:512 * hp + 512],
                        start=True, stop=True, tile_position=(0, 0)),
                        reads=[ktkey, f"qTb_{hp}"], writes=[pk(b0)])
                    kb.op("tensor", lambda e, kt=kt, tc=tc, b1=b1, hp=hp: e.matmul(
                        pf(b1), kt[64:128, 128 * tc:128 * tc + 128],
                        qTb[64:128, 512 * hp:512 * hp + 512],
                        start=True, stop=True, tile_position=(64, 0)),
                        reads=[ktkey, f"qTb_{hp}"], writes=[pk(b1)])
                    if len(pend_av) == 2:
                        pend_av.pop(0)()
                    kb.op("scalar", lambda e, w0=w0, hp=hp, b0=b0: e.activation(
                        w0, pf(b0), AF.Exp, bias=negBb[:, 2 * hp:2 * hp + 1],
                        scale=1.0 / SCALE),
                        reads=[pk(b0), "negBb"], writes=[k0])
                    kb.op("scalar", lambda e, w1=w1, hp=hp, b1=b1: e.activation(
                        w1, pf(b1), AF.Exp, bias=negBb[:, 2 * hp + 1:2 * hp + 2],
                        scale=1.0 / SCALE),
                        reads=[pk(b1), "negBb"], writes=[k1])
                    pend_av.append(mk_av(tc, w0=w0, w1=w1, k0=k0, k1=k1))
                for go in pend_av:
                    go()
                pend_av = []
                kb.op("vector", lambda e, hp=hp: e.tensor_copy(
                    fUT[0:64, 512 * hp:512 * hp + 512], pf(4)[0:64, :]),
                    reads=[pk(4)], writes=[f"fu_{hp}"])
                kb.op("vector", lambda e, hp=hp: e.tensor_copy(
                    fUT[64:128, 512 * hp:512 * hp + 512], pf(5)[0:64, :]),
                    reads=[pk(5), f"fu_{hp}"], writes=[f"fu_{hp}"])
                kb.op("vector", lambda e: e.tensor_copy(
                    p1a[64:65, 0:512], pf(4)[64:65, :]),
                    reads=[pk(4)], writes=["p1a"])
                kb.op("vector", lambda e: e.tensor_copy(
                    p1a[64:65, 512:1024], pf(5)[64:65, :]),
                    reads=[pk(5), "p1a"], writes=["p1a"])
                for half in range(2):
                    h = 2 * hp + half
                    for sc in range(4):
                        kb.op("tensor", lambda e, half=half, sc=sc, h=h: e.transpose(
                            pbf(6)[:, 2 * (32 * sc + h):2 * (32 * sc + h) + 1],
                            p1a[64:65, 512 * half + 128 * sc:
                                512 * half + 128 * sc + 128],
                            ident_s[64:65, 64:65]),
                            reads=["p1a", "ident"], writes=[pk(6)])
                gu_step(1, banks=(2, 3))
                if hp + 2 < HP:
                    kb.dma("sync", kTs[hp % 2], kT_d[hp + 2, :, :], f"kTss{hp % 2}",
                           reads=[f"kTd_{hp + 2}_{s2}" for s2 in range(4)],
                           writes=[ktkey])
            kb.op("vector", lambda e: e.reciprocal(
                ZtR[:, 0:128], pbf(6)[:, 0:256:2]),
                reads=[pk(6)], writes=["ZtR"])
            for sc in range(4):
                for hp in range(HP):
                    bank = 6 + hp % 2
                    kb.op("tensor", lambda e, hp=hp, sc=sc, bank=bank: e.transpose(
                        pbf(bank)[:, 0:128],
                        fUT[:, 512 * hp + 128 * sc:512 * hp + 128 * sc + 128],
                        ident_s[:]),
                        reads=[f"fu_{hp}", "ident"], writes=[pk(bank)])
                    for half in range(2):
                        h = 2 * hp + half
                        kb.op("vector", lambda e, h=h, sc=sc, half=half, bank=bank:
                              e.tensor_tensor(
                                  p1d[:, 64 * h:64 * h + 64],
                                  pbf(bank)[:, 64 * half:64 * half + 64],
                                  ZtR[:, 32 * sc + h:32 * sc + h + 1].broadcast_to(
                                      [128, 64]),
                                  OP.mult),
                              reads=[pk(bank), "ZtR"], writes=[PXK[4 + h // 8]])
                for jc in range(JC):
                    bank = 6 + jc % 2
                    kb.op("tensor", lambda e, jc=jc, bank=bank: e.transpose(
                        pbf(bank)[:, 0:128], p1d[:, 128 * jc:128 * jc + 128],
                        ident_s[:]),
                        reads=[PXK[4 + jc // 4], "ident"], writes=[pk(bank)])
                    kb.op("vector", lambda e, jc=jc, sc=sc, bank=bank: e.tensor_copy(
                        lnTs2[:, 512 * jc + 128 * sc:512 * jc + 128 * sc + 128],
                        pbf(bank)[:, 0:128]),
                        reads=[pk(bank)], writes=[f"l2_{jc}"])
            attS = [p1b, p1d]
            attK = [["p1b"], P1DK]
            kb.dma("gpsimd", borow, bo.unsqueeze(0), "borow", writes=BRK)
            for sbp in range(2):
                for jc in range(JC):
                    wo, wokeys = wslot[2 + jc % 2], qkeys(2 + jc % 2)
                    kb.dma("sync" if jc % 2 == 0 else "gpsimd", wo,
                           wqkvo_d[3, 128 * jc:128 * jc + 128, :],
                           f"wos{jc % 2}", reads=[f"wc_3_{jc}"], writes=wokeys)

                    def mmo(e, wo=wo, jc=jc, sbp=sbp):
                        for i in range(2):
                            sb_ = 2 * sbp + i
                            for js in range(4):
                                ins = e.matmul(
                                    pf(2 * js + i),
                                    lnTs2[:, 512 * jc + 128 * sb_:
                                          512 * jc + 128 * sb_ + 128],
                                    wo[:, 512 * js:512 * js + 512],
                                    start=(jc == 0), stop=False)
                        return ins
                    kb.op("tensor", mmo,
                          reads=wokeys + [f"l2_{jc2}" for jc2 in range(JC)],
                          writes=[pk(b) for b in range(8)])

                def mmob(e):
                    for i in range(2):
                        for js in range(4):
                            ins = e.matmul(pf(2 * js + i), ones_s[0:1, :],
                                           borow[0:1, 512 * js:512 * js + 512],
                                           start=False, stop=True)
                    return ins
                kb.op("tensor", mmob, reads=["ones"] + BRK,
                      writes=[pk(b) for b in range(8)])
                for i in range(2):
                    for js in range(4):
                        kb.op("vector", lambda e, i=i, js=js: e.tensor_scalar(
                            attS[i][:, 512 * js:512 * js + 512], pf(2 * js + i),
                            1.0, 1.0, OP.mult, OP.mult),
                            reads=[pk(2 * js + i)], writes=attK[i])
                for i in range(2):
                    sb_ = 2 * sbp + i
                    kb.op("vector", lambda e, i=i: e.scalar_tensor_tensor(
                        o1[:], attS[i], 1.0, attS[i], OP.mult, OP.max,
                        accum_out=stats[:, 16:17]),
                        reads=attK[i], writes=["o1", "st16"])
                    kb.op("vector", lambda e, i=i: e.scalar_tensor_tensor(
                        o1[:], attS[i], 1.0, attS[i], OP.mult, OP.mult,
                        accum_out=stats[:, 17:18]),
                        reads=attK[i] + ["o1"], writes=["o1", "st17"])
                    ln_from_sums(16, 17, 18, 19)
                    kb.op("vector", lambda e: e.scalar_tensor_tensor(
                        stats[:, 20:21], stats[:, 16:17], 1.0, stats[:, 17:18],
                        OP.mult, OP.mult),
                        reads=["st16", "st17"], writes=["st20"])
                    xkey = f"xf{i}"
                    trow = 512 * sup + 128 * sb_
                    kb.dma("gpsimd", xf[i][:], x_in[seq, trow:trow + 128, :],
                           f"xfs{i}", writes=[xkey])
                    kb.op("vector", lambda e, i=i: e.tensor_scalar(
                        o1[:], xf[i][:], stats[:, 20:21], 1.0, OP.subtract, OP.mult),
                        reads=[xkey, "st20"], writes=["o1"])
                    kb.op("vector", lambda e, i=i: e.scalar_tensor_tensor(
                        xf[i][:], attS[i], stats[:, 17:18], o1[:], OP.mult, OP.add),
                        reads=attK[i] + ["st17", "o1"], writes=[xkey])
                    kb.dma("scalar", x1_d[2048 * seq + trow:2048 * seq + trow + 128, :],
                           xf[i][:], "x1_out", reads=[xkey],
                           writes=[f"x1d_{(2048 * seq + trow) // 128}"])

    def ffn_tg(tg):
        kb.new_phase()
        rowbase = 512 * tg
        for tb in range(TBS):
            bi = tb % 2
            xkey = f"xf{bi}"
            kb.dma("sync", xf[bi][:], x1_d[rowbase + 128 * tb:rowbase + 128 * tb + 128, :],
                   f"xfs{bi}", reads=[f"x1d_{(rowbase + 128 * tb) // 128}"],
                   writes=[xkey])
            cb = 4 * bi
            jk = [("o1",), ("p1b",)][bi]
            jap = [o1[:], p1b][bi]
            ln_stats(xkey, xf[bi][:], cb, cb + 1, cb + 2, cb + 3, jap, jk)
            kb.op("vector", lambda e, bi=bi, cb=cb: e.tensor_scalar(
                p1a[:, 0:1024], xf[bi][:, 0::2], stats[:, cb:cb + 1],
                stats[:, cb + 1:cb + 2], OP.subtract, OP.mult),
                reads=[xkey, f"st{cb}", f"st{cb + 1}"], writes=["p1a"])
            kb.op("vector", lambda e, bi=bi, cb=cb: e.tensor_scalar(
                p1a[:, 1024:2048], xf[bi][:, 1::2], stats[:, cb:cb + 1],
                stats[:, cb + 1:cb + 2], OP.subtract, OP.mult),
                reads=[xkey, f"st{cb}", f"st{cb + 1}", "p1a"], writes=["p1a"])
            for fc in range(FC):
                bank = 4 + fc % 2
                kb.op("tensor", lambda e, fc=fc, bank=bank: e.transpose(
                    pbf(bank)[:, 0:128], p1a[:, 128 * fc:128 * fc + 128], ident_s[:]),
                    reads=["p1a", "ident"], writes=[pk(bank)])
                kb.op("vector", lambda e, fc=fc, tb=tb, bank=bank: e.tensor_copy(
                    ln2Ts[:, 512 * fc + 128 * tb:512 * fc + 128 * tb + 128],
                    pbf(bank)[:, 0:128]),
                    reads=[pk(bank)], writes=[f"ln2_{fc}"])
        for nh in range(2):
            for c in range(NCH):
                gc = NCH * nh + c
                slb, slkeys = wgu[c % 2], WGUK[c % 2]
                kb.dma("sync", slb.rearrange("p (a b) -> p a b", a=2),
                       wguT_d[gc].rearrange("p (a b) -> p a b", a=2),
                       f"wgus{c % 2}", reads=[f"wguT_{gc}"], writes=slkeys)
                bg, bu = [(0, 1), (2, 3), (4, 5)][c % 3]

                def mmg(e, slb=slb, bg=bg):
                    for fc in range(FC):
                        ins = e.matmul(pf(bg), slb[:, 128 * fc:128 * fc + 128],
                                       ln2Ts[:, 512 * fc:512 * fc + 512],
                                       start=(fc == 0), stop=(fc == FC - 1))
                    return ins
                kb.op("tensor", mmg,
                      reads=slkeys + [f"ln2_{fc}" for fc in range(FC)],
                      writes=[pk(bg)])

                def mmu(e, slb=slb, bu=bu):
                    for fc in range(FC):
                        ins = e.matmul(pf(bu), slb[:, 2048 + 128 * fc:2048 + 128 * fc + 128],
                                       ln2Ts[:, 512 * fc:512 * fc + 512],
                                       start=(fc == 0), stop=(fc == FC - 1))
                    return ins
                kb.op("tensor", mmu,
                      reads=slkeys + [f"ln2_{fc}" for fc in range(FC)],
                      writes=[pk(bu)])
                kb.op("scalar", lambda e, bg=bg: e.copy(pex[0], pf(bg)),
                      reads=[pk(bg)], writes=["px0"])
                kb.op("vector", lambda e: e.tensor_tensor(
                    pex[1], pex[0], pex[0], OP.mult),
                    reads=["px0"], writes=["px1"])
                kb.op("vector", lambda e: e.tensor_scalar(
                    pex[2], pex[1], GC1, GC0, OP.mult, OP.add),
                    reads=["px1"], writes=["px2"])
                kb.op("vector", lambda e: e.tensor_tensor(
                    pex[3], pex[2], pex[0], OP.mult),
                    reads=["px2", "px0"], writes=["px3"])
                kb.op("scalar", lambda e: e.activation(
                    pex[1], pex[3], AF.Sigmoid),
                    reads=["px3"], writes=["px1"])
                kb.op("vector", lambda e: e.tensor_tensor(
                    pex[2], pex[1], pex[0], OP.mult),
                    reads=["px1", "px0"], writes=["px2"])
                kb.op("vector", lambda e, c=c, bu=bu: e.tensor_tensor(
                    interT[:, 512 * c:512 * c + 512], pex[2], pf(bu), OP.mult),
                    reads=["px2", pk(bu)], writes=[f"it_{c}"])
            for hh in range(2):
                for c in range(NCH):
                    gc = NCH * nh + c
                    dnb, dnkeys = wdn[c % 4], WDNK[c % 4]
                    kb.dma("sync", dnb, wdT_d[gc][:, 1024 * hh:1024 * hh + 1024],
                           f"wdns{c % 4}", reads=[f"wdT_{gc}"], writes=dnkeys)

                    def mmd(e, dnb=dnb, c=c):
                        for tb in range(4):
                            for hs in range(2):
                                ins = e.matmul(
                                    pf(2 * tb + hs),
                                    interT[:, 512 * c + 128 * tb:512 * c + 128 * tb + 128],
                                    dnb[:, 512 * hs:512 * hs + 512],
                                    start=(c == 0), stop=(c == NCH - 1))
                        return ins
                    kb.op("tensor", mmd,
                          reads=dnkeys + [f"it_{c}"],
                          writes=[pk(i) for i in range(8)])
                for tb in range(4):
                    kb.op("vector", lambda e, tb=tb: e.tensor_copy(
                        o1[:, 0:512], pf(2 * tb)),
                        reads=[pk(2 * tb)], writes=["o1"])
                    kb.op("vector", lambda e, tb=tb: e.tensor_copy(
                        o1[:, 512:1024], pf(2 * tb + 1)),
                        reads=[pk(2 * tb + 1), "o1"], writes=["o1"])
                    r0 = rowbase + 128 * tb
                    if nh == 0:
                        kb.dma("sync", x2p_d[r0:r0 + 128, 1024 * hh:1024 * hh + 1024],
                               o1[:, 0:1024], f"x2p_out{tb % 2}", reads=["o1"],
                               writes=[f"x2p_{tb}_{hh}_{tg}"])
                    else:
                        kb.dma("gpsimd", x2p_d[r0:r0 + 128, 1024 * hh:1024 * hh + 1024],
                               o1[:, 0:1024], f"x2pa{tb % 2}",
                               reads=["o1", f"x2p_{tb}_{hh}_{tg}"],
                               writes=[f"x2p_{tb}_{hh}_{tg}"],
                               accum_op=OP.add)
        for tb in range(TBS):
            r0 = rowbase + 128 * tb
            kb.dma("sync", xf[0][:], x2p_d[r0:r0 + 128, :], "x2p_in",
                   reads=[f"x2p_{tb}_{hh}_{tg}" for hh in range(2)], writes=["xf0"])
            kb.dma("sync", xf[1][:], x1_d[r0:r0 + 128, :], "x1_in2",
                   reads=[f"x1d_{r0 // 128}"], writes=["xf1"])
            ln_stats("xf0", xf[0][:], 16, 17, 18, 19)
            kb.op("vector", lambda e: e.scalar_tensor_tensor(
                stats[:, 20:21], stats[:, 16:17], 1.0, stats[:, 17:18],
                OP.mult, OP.mult),
                reads=["st16", "st17"], writes=["st20"])
            kb.op("vector", lambda e: e.tensor_scalar(
                o1[:], xf[1][:], stats[:, 20:21], 1.0, OP.subtract, OP.mult),
                reads=["xf1", "st20"], writes=["o1"])
            kb.op("vector", lambda e: e.scalar_tensor_tensor(
                xf[1][:], xf[0][:], stats[:, 17:18], o1[:], OP.mult, OP.add),
                reads=["xf0", "st17", "o1"], writes=["xf1"])
            seq = r0 // 2048
            srow = r0 % 2048
            kb.dma("sync", out[seq, srow:srow + 128, :], xf[1][:], "outp",
                   reads=["xf1"], writes=[f"out_{r0 // 128}"])

    for seq in range(BC):
        attn_phaseA(seq)
        attn_phaseB(seq)
        gu_flush(banks=(2, 3))
    assert down_state["g"] == 512, down_state["g"]
    assert gu_state["pe"] == N_GU and gu_state["dve"] == N_GU, gu_state
    kb.full_barrier()
    kb.new_phase()
    for tg in range(TG):
        ffn_tg(tg)
    kb.finalize(done_scr[:])

    with nc.Block() as block:
        @block.tensor
        def _(e):
            for f in kb.stream["tensor"]:
                f(e)

        @block.vector
        def _(e):
            for f in kb.stream["vector"]:
                f(e)

        @block.scalar
        def _(e):
            for f in kb.stream["scalar"]:
                f(e)

        @block.gpsimd
        def _(e):
            for f in kb.stream["gpsimd"]:
                f(e)

        @block.sync
        def _(e):
            for f in kb.stream["sync"]:
                f(e)

    return nc


_NC_CACHE = {}


def kernel(**inputs):
    x = np.ascontiguousarray(inputs["L0_attn_x"])
    if "nc" not in _NC_CACHE:
        _NC_CACHE["nc"] = build_nc()
    nc = _NC_CACHE["nc"]
    wkeys = ["L0_attn_Wq", "L0_attn_bq", "L0_attn_Wk", "L0_attn_bk",
             "L0_attn_Wv", "L0_attn_bv", "L0_attn_Wo", "L0_attn_bo",
             "L0_ffn_gate_W_packed", "L0_ffn_gate_scales",
             "L0_ffn_up_W_packed", "L0_ffn_up_scales",
             "L0_ffn_down_W_packed", "L0_ffn_down_scales"]
    shared = {k: np.ascontiguousarray(inputs[k]) for k in wkeys}
    in_maps = []
    for c in range(8):
        m = dict(shared)
        m["L0_attn_x"] = np.ascontiguousarray(x[2 * c:2 * c + 2])
        in_maps.append(m)
    res = run_bass_kernel_spmd(nc, in_maps, core_ids=list(range(8)))
    return np.concatenate([r["out"] for r in res.results], axis=0)


# version 14
# speedup vs baseline: 1.1941x; 1.0868x over previous
import numpy as np
import ml_dtypes
import concourse.bass as bass
import concourse.mybir as mybir
from concourse.bass_utils import run_bass_kernel_spmd

F32 = mybir.dt.float32
BF16 = mybir.dt.bfloat16
I32 = mybir.dt.int32
OP = mybir.AluOpType
AF = mybir.ActivationFunctionType
AX = mybir.AxisListType
bfdt = ml_dtypes.bfloat16

BC = 2
S = 2048
Fd = 2048
NHEAD = 32
NMLP = 16384
SCALE = float(np.sqrt(32.0))
FC = 16
JC = 16
NSUP = 4
TBS = 4
HP = 16
TCH = 16
TG = 8
NCH = 64
TOK = BC * S
GC0 = 1.5957691216057308
GC1 = 2.0 * 0.7978845608028654 * 0.044715

ENGINES = ("tensor", "vector", "scalar", "gpsimd", "sync")
DTSZ = {F32: 4, BF16: 2, I32: 4}


class KB:

    def __init__(self, nc):
        self.nc = nc
        self.stream = {e: [] for e in ENGINES}
        self.sem_h = {}
        self.sem_count = {}
        self.eng_waited = {e: {} for e in ENGINES}
        self.res = {}
        self.phase = 0
        self.esem = {}
        self.new_phase()

    def _sem(self, name):
        if name not in self.sem_h:
            self.sem_h[name] = self.nc.alloc_semaphore(name=name)
            self.sem_count[name] = 0
        return self.sem_h[name]

    def new_phase(self):
        self.phase += 1
        self.esem = {e: f"E{self.phase % 3}_{e}" for e in ENGINES}
        for e in ENGINES:
            self._sem(self.esem[e])

    def _wait(self, eng, semname, val):
        if val <= 0:
            return
        assert val <= self.sem_count[semname], (semname, val, self.sem_count[semname])
        w = self.eng_waited[eng]
        if w.get(semname, 0) >= val:
            return
        w[semname] = val
        h = self.sem_h[semname]
        self.stream[eng].append(lambda e, h=h, v=val: e.wait_ge(h, v))

    def _deps(self, eng, reads, writes):
        for k in reads:
            rec = self.res.get(k)
            if rec and rec["w"]:
                self._wait(eng, *rec["w"])
        for k in writes:
            rec = self.res.get(k)
            if rec:
                if rec["w"]:
                    self._wait(eng, *rec["w"])
                for (sn, v) in rec["r"]:
                    self._wait(eng, sn, v)

    def _record(self, token, reads, writes):
        for k in writes:
            self.res[k] = {"w": token, "r": []}
        for k in reads:
            self.res.setdefault(k, {"w": None, "r": []})["r"].append(token)

    def op(self, eng, fn, reads=(), writes=()):
        self._deps(eng, reads, writes)
        semname = self.esem[eng]
        self.sem_count[semname] += 1
        h = self.sem_h[semname]

        def emit(e, fn=fn, h=h):
            fn(e).then_inc(h, 1)
        self.stream[eng].append(emit)
        token = (semname, self.sem_count[semname])
        self._record(token, reads, writes)
        return token

    def dma(self, queue, out, in_, sname, reads=(), writes=(), **kw):
        self._deps(queue, reads, writes)
        semname = "D_" + sname
        self._sem(semname)
        self._wait(queue, semname, self.sem_count[semname])
        self.sem_count[semname] += 16
        h = self.sem_h[semname]

        def emit(e, out=out, in_=in_, h=h, kw=kw):
            e.dma_start(out=out, in_=in_, **kw).then_inc(h, 16)
        self.stream[queue].append(emit)
        token = (semname, self.sem_count[semname])
        self._record(token, reads, writes)
        return token

    def full_barrier(self):
        counts = [(n, c) for n, c in self.sem_count.items() if c > 0]
        for e in ENGINES:
            for n, c in counts:
                self._wait(e, n, c)
        self.res = {}

    def finalize(self, done_scratch):
        done = self._sem("DONE")
        for e in ("vector", "scalar", "gpsimd"):
            def emit(en, d=done, scr=done_scratch):
                en.memzero(scr).then_inc(d, 1)
            self.stream[e].append(emit)
        self.sem_count["DONE"] = 3
        for name, cnt in list(self.sem_count.items()):
            if cnt > 0 and name != "DONE":
                self._wait("sync", name, cnt)
        self.stream["sync"].append(lambda e, h=done: e.wait_ge(h, 3))
        handles = list(self.sem_h.values())

        def clear(e, hs=handles):
            for hh in hs:
                e.sem_clear(hh)
        self.stream["sync"].append(clear)


def gu_row_start(c):
    j = c % 4
    return 512 * (c // 4) + (j % 2) * 256 + (j // 2)


def build_nc():
    nc = bass.Bass("TRN2", target_bir_lowering=False)

    x_in = nc.dram_tensor("L0_attn_x", [BC, S, Fd], F32, kind="ExternalInput").ap()
    Wq = nc.dram_tensor("L0_attn_Wq", [Fd, Fd], F32, kind="ExternalInput").ap()
    bq = nc.dram_tensor("L0_attn_bq", [Fd], F32, kind="ExternalInput").ap()
    Wk = nc.dram_tensor("L0_attn_Wk", [Fd, Fd], F32, kind="ExternalInput").ap()
    bk = nc.dram_tensor("L0_attn_bk", [Fd], F32, kind="ExternalInput").ap()
    Wv = nc.dram_tensor("L0_attn_Wv", [Fd, Fd], F32, kind="ExternalInput").ap()
    bv = nc.dram_tensor("L0_attn_bv", [Fd], F32, kind="ExternalInput").ap()
    Wo = nc.dram_tensor("L0_attn_Wo", [Fd, Fd], F32, kind="ExternalInput").ap()
    bo = nc.dram_tensor("L0_attn_bo", [Fd], F32, kind="ExternalInput").ap()
    gpk = nc.dram_tensor("L0_ffn_gate_W_packed", [NMLP, Fd // 2], I32, kind="ExternalInput").ap()
    gsc = nc.dram_tensor("L0_ffn_gate_scales", [NMLP, Fd // 32], F32, kind="ExternalInput").ap()
    upk = nc.dram_tensor("L0_ffn_up_W_packed", [NMLP, Fd // 2], I32, kind="ExternalInput").ap()
    usc = nc.dram_tensor("L0_ffn_up_scales", [NMLP, Fd // 32], F32, kind="ExternalInput").ap()
    dpk = nc.dram_tensor("L0_ffn_down_W_packed", [Fd, NMLP // 2], I32, kind="ExternalInput").ap()
    dsc = nc.dram_tensor("L0_ffn_down_scales", [Fd, NMLP // 32], F32, kind="ExternalInput").ap()
    out = nc.dram_tensor("out", [BC, S, Fd], F32, kind="ExternalOutput").ap()

    lnT_d = nc.dram_tensor("lnT_d", [Fd, S], BF16, kind="Internal").ap()
    kT_d = nc.dram_tensor("kT_d", [JC, 128, S], BF16, kind="Internal").ap()
    x1_d = nc.dram_tensor("x1_d", [TOK, Fd], F32, kind="Internal").ap()
    x2p_d = nc.dram_tensor("x2p_d", [TOK, Fd], F32, kind="Internal").ap()
    wguT_d = nc.dram_tensor("wguT_d", [128, 128, 2 * Fd], BF16, kind="Internal").ap()
    wqkvo_d = nc.dram_tensor("wqkvo_d", [4, Fd, Fd], BF16, kind="Internal").ap()
    wqkT_d = nc.dram_tensor("wqkT_d", [2, JC, 128, Fd], BF16, kind="Internal").ap()
    wdT_d = nc.dram_tensor("wdT_d", [128, 128, Fd], BF16, kind="Internal").ap()

    ident = nc.inline_tensor(np.eye(128, dtype=bfdt), "ident")
    ones_row = nc.inline_tensor(np.ones((1, 128), dtype=bfdt), "ones_row")
    E_np = np.zeros((JC, 128, NHEAD), dtype=bfdt)
    for jc in range(JC):
        E_np[jc, 0:64, 2 * jc] = 1
        E_np[jc, 64:128, 2 * jc + 1] = 1
    E_d = nc.inline_tensor(E_np.reshape(JC * 128, NHEAD), "E_ind")

    def sb(name, shape, dtype):
        return nc.alloc_sbuf_tensor(name, shape, dtype)

    ident_s = sb("ident_s", [128, 128], BF16)
    ones_s = sb("ones_s", [1, 128], BF16)
    E_s = sb("E_s", [128, JC * NHEAD], BF16)
    bq_t = sb("bq_t", [128, JC], F32)
    bk_t = sb("bk_t", [128, JC], F32)

    Brow = sb("Brow", [1, 32], BF16)
    Bcol = sb("Bcol", [128, 1], BF16)
    eps_c = sb("eps_c", [128, 1], F32)
    stats = sb("stats", [128, 24], F32)
    negBb = sb("negBb", [128, NHEAD], F32)
    ZtR = sb("ZtR", [128, 128], F32)
    mkq = sb("mkq", [128, 8], F32)
    done_scr = sb("done_scr", [1, 1], F32)
    DQC = 512
    stage = [sb(f"stage{i}", [128, 2048], BF16) for i in range(4)]
    xf = [sb(f"xf{i}", [128, Fd], F32) for i in range(2)]
    o1 = sb("o1", [128, Fd], F32)
    p1 = sb("p1", [128, 8192], BF16)
    wp = sb("wp", [128, 8192], BF16)

    p1a = p1[:, 0:2048]
    p1b = p1[:, 2048:4096]
    pex = [p1[:, 4096 + 512 * i:4096 + 512 * i + 512] for i in range(8)]
    PXK = [f"px{i}" for i in range(8)]
    p1d = p1[:, 6144:8192]
    P1DK = ["px4", "px5", "px6", "px7"]
    bvrow = p1[0:1, 4096:6144]
    borow = p1[0:1, 4096:6144]
    BRK = ["px0", "px1", "px2", "px3"]

    wslot = [wp[:, 2048 * i:2048 * i + 2048] for i in range(4)]

    def qkeys(i):
        return [f"Q{i}{c}" for c in "abcd"]

    wvbuf = [wp[:, 4096:4608], wp[:, 6144:6656], wp[:, 4608:5120], wp[:, 6656:7168]]
    WVK = [["Q2a"], ["Q3a"], ["Q2b"], ["Q3b"]]
    wgu = [wp[:, 0:4096], wp[:, 4096:8192]]
    WGUK = [qkeys(0) + qkeys(1), qkeys(2) + qkeys(3)]
    wdn = [wp[:, 0:1024], wp[:, 2048:3072], wp[:, 4096:5120], wp[:, 6144:7168]]
    WDNK = [["Q0a", "Q0b"], ["Q1a", "Q1b"], ["Q2a", "Q2b"], ["Q3a", "Q3b"]]

    ZONE = nc.sbuf_base + 32
    ZEND = nc.sbuf_top

    def at(name, shape, dtype, off):
        h = nc.alloc_sbuf_tensor_at(name, shape, dtype, offset=off)
        return h, off + int(np.prod(shape[1:])) * DTSZ[dtype]

    az = ZONE
    vb, az = at("vb", [128, 16 * NHEAD * 65], BF16, az)
    qTb, az = at("qTb", [128, JC * 512], BF16, az)
    fUT, az = at("fUT", [128, HP * 512], BF16, az)
    lnTs2, az = at("lnTs2", [128, FC * 512], BF16, az)
    kTs_t, az = at("kTs", [128, 2 * S], BF16, az)
    fz = ZONE
    interT, fz = at("interT", [128, NCH * 512], BF16, fz)
    ln2Ts, fz = at("ln2Ts", [128, FC * 512], BF16, fz)
    assert az >= fz, "dequant buffers must start above the FFN zone"
    pkb = []
    for i in range(2):
        h, az = at(f"pkb{i}", [128, 512], I32, az)
        pkb.append(h)
    dqt = []
    for i in range(2):
        h, az = at(f"dqt{i}", [128, DQC], I32, az)
        dqt.append(h)
    wnat = []
    for i in range(3):
        h, az = at(f"wnat{i}", [128, 512], BF16, az)
        wnat.append(h)
    scl = []
    for i in range(2):
        h, az = at(f"scl{i}", [128, 64], F32, az)
        scl.append(h)
    dscl, az = at("dscl", [128, 64], F32, az)
    assert az <= ZEND, f"attention zone overflow: {az} > {ZEND}"

    lnTs = fUT
    kTs = [kTs_t[:, 0:2048], kTs_t[:, 2048:4096]]

    kb = KB(nc)

    psg = [nc.psum_tensor(f"pb{i}", [128, 512], F32) for i in range(8)]
    pst = [g.__enter__() for g in psg]

    def pf(i, n=512):
        return pst[i][:, 0:n]

    def pbf(i):
        return pst[i][:].bitcast(BF16)

    def pk(i):
        return f"pb{i}"

    kb.dma("sync", bass.AP(ident_s, 0, [[128, 128], [1, 128]]), ident.ap(), "pre",
           writes=["ident"])
    kb.dma("sync", bass.AP(ones_s, 0, [[128, 1], [1, 128]]), ones_row.ap(), "pre",
           writes=["ones"])
    kb.dma("sync", E_s[:].rearrange("p (jc h) -> p jc h", jc=JC),
           E_d.ap().rearrange("(jc p) h -> p jc h", p=128), "pre", writes=["E"])
    for jc in range(JC):
        kb.dma("sync", bq_t[:, jc:jc + 1],
               bq[128 * jc:128 * jc + 128].rearrange("(p one) -> p one", one=1),
               "pre", writes=["bq"])
        kb.dma("sync", bk_t[:, jc:jc + 1],
               bk[128 * jc:128 * jc + 128].rearrange("(p one) -> p one", one=1),
               "pre", writes=["bk"])
    kb.op("gpsimd", lambda e: e.memset(eps_c[:], 1e-5), writes=["eps"])
    ones_view = bass.AP(vb, 64, [[16 * NHEAD * 65, 128], [NHEAD * 65, 16], [65, NHEAD]])
    kb.op("gpsimd", lambda e: e.memset(ones_view, 1.0), writes=["vbones"])
    for m, Wsrc in ((1, Wk), (2, Wv), (0, Wq), (3, Wo)):
        role = {1: 0, 0: 1}.get(m)
        for fc in range(FC):
            qslot = fc % 4
            buf = wslot[qslot]
            kb.dma("gpsimd", buf, Wsrc[128 * fc:128 * fc + 128, :],
                   f"pcin{qslot}", writes=qkeys(qslot))
            if role is not None:
                kb.dma("sync",
                       wqkT_d[role][:, :, 128 * fc:128 * fc + 128].rearrange(
                           "jc p j -> p jc j"),
                       buf.rearrange("p (jc j) -> p jc j", jc=JC),
                       f"pcout{qslot}", reads=qkeys(qslot),
                       writes=[f"wb_{role}_{fc}"])
            else:
                kb.dma("sync", wqkvo_d[m, 128 * fc:128 * fc + 128, :], buf,
                       f"pcout{qslot}", reads=qkeys(qslot), writes=[f"wc_{m}_{fc}"])

    P126 = float(2.0 ** 126)
    PKK = [[f"pkb{i}a", f"pkb{i}b"] for i in range(2)]
    DQTK = [[f"dqt{i}a", f"dqt{i}b"] for i in range(2)]

    def dq_nibble(pap, pkkeys, sckey, sbc, dst_ap, dstkey, lohi, t, tkeys):
        sh = 28 if lohi == 0 else 24
        n = pap.shape[1]
        kb.op("vector", lambda e: e.tensor_scalar(
            t[:, 0:n], pap, sh, 6, OP.logical_shift_left, OP.arith_shift_right),
            reads=pkkeys, writes=tkeys)
        kb.op("vector", lambda e: e.tensor_scalar(
            t[:, 0:n], t[:, 0:n], 0x81C00000 - 2 ** 32, 0, OP.bitwise_and, OP.bitwise_or),
            reads=tkeys, writes=tkeys)
        kb.op("gpsimd", lambda e: e.tensor_tensor(
            dst_ap.rearrange("p (b i) -> p b i", i=16),
            t[:, 0:n].bitcast(F32).rearrange("p (b i) -> p b i", i=16),
            sbc, OP.mult),
            reads=tkeys + [sckey], writes=[dstkey])

    dq_state = {"banks": (0, 1)}

    def emit_gu_slab_dve(c):
        rs = gu_row_start(c)
        for half, (pk_src, sc_src) in enumerate(((gpk, gsc), (upk, usc))):
            wns = stage[2 + half]
            wnkey = f"stage{2 + half}"
            sbuf_ = scl[half]
            sckey = f"dqsc{half}"
            kb.dma("gpsimd", sbuf_[:], sc_src[rs:rs + 255:2, :], f"dqscs{half}",
                   writes=[sckey])
            kb.op("vector", lambda e, s=sbuf_: e.tensor_scalar(
                s[:], s[:], P126, 1.0, OP.mult, OP.mult),
                reads=[sckey], writes=[sckey])
            for pl in range(2):
                pbuf = pkb[pl]
                kb.dma("gpsimd", pbuf[:],
                       pk_src[rs:rs + 255:2, 512 * pl:512 * pl + 512],
                       f"dqpks{pl}", writes=PKK[pl])
                pap = pbuf[:]
                sbc = sbuf_[:, 32 * pl:32 * pl + 32].unsqueeze(2).broadcast_to(
                    [128, 32, 16])
                for lohi in range(2):
                    dq_nibble(pap, PKK[pl], sckey, sbc,
                              wns[:, 1024 * lohi + 512 * pl:
                                  1024 * lohi + 512 * pl + 512], wnkey, lohi,
                              dqt[lohi], DQTK[lohi])

    def emit_gu_slab_pe(c):
        for half in range(2):
            wns = stage[2 + half]
            wnkey = f"stage{2 + half}"
            stg = stage[half]
            stgkey = f"stage{half}"
            bank = dq_state["banks"][half]
            for grp in range(4):
                for sub in range(4):
                    fcid = 4 * grp + sub
                    kb.op("tensor", lambda e, fcid=fcid, sub=sub, wns=wns, bank=bank:
                          e.transpose(
                              pbf(bank)[:, 128 * sub:128 * sub + 128],
                              wns[:, 128 * fcid:128 * fcid + 128], ident_s[:]),
                          reads=[wnkey, "ident"], writes=[pk(bank)])
                kb.op("vector", lambda e, grp=grp, stg=stg, bank=bank: e.tensor_copy(
                    stg[:, 512 * grp:512 * grp + 512], pbf(bank)[:, 0:512]),
                    reads=[pk(bank)], writes=[stgkey])
            kb.dma("sync", wguT_d[c, :, 2048 * half:2048 * half + 2048], stg[:],
                   "wguT_out", reads=[stgkey], writes=[f"wguT_{c}"])

    N_GU = 128
    gu_state = {"dve": 0, "pe": 0}

    def gu_step(n=1, banks=None):
        if banks is not None:
            dq_state["banks"] = banks
        for _ in range(n):
            if gu_state["dve"] > gu_state["pe"]:
                emit_gu_slab_pe(gu_state["pe"])
                gu_state["pe"] += 1
            if gu_state["dve"] < N_GU:
                emit_gu_slab_dve(gu_state["dve"])
                gu_state["dve"] += 1

    def gu_flush(banks=None):
        if banks is not None:
            dq_state["banks"] = banks
        while gu_state["pe"] < gu_state["dve"]:
            emit_gu_slab_pe(gu_state["pe"])
            gu_state["pe"] += 1

    down_state = {"g": 0, "gl": 0, "pend": []}

    def down_emit_load(gl):
        u, hc = gl // 16, gl % 16
        s4 = gl % 4
        pkv = pkb[s4 // 2][:, 256 * (s4 % 2):256 * (s4 % 2) + 256]
        kb.dma("gpsimd", pkv,
               dpk[128 * hc:128 * hc + 128, 256 * u:256 * u + 256],
               f"ddpk{s4}", writes=[PKK[s4 // 2][s4 % 2]])
        scv = dscl[:, 16 * s4:16 * s4 + 16]
        kb.dma("gpsimd", scv,
               dsc[128 * hc:128 * hc + 128, 16 * u:16 * u + 16],
               f"ddsc{s4}", writes=[f"dscl{s4}"])

    def down_micro(n=1, banks=None):
        if banks is not None:
            dq_state["banks"] = banks
        for _ in range(n):
            g = down_state["g"]
            if g >= 512:
                return
            while down_state["gl"] < min(512, g + 4, (g // 256 + 1) * 256):
                down_emit_load(down_state["gl"])
                down_state["gl"] += 1
            u, hc = g // 16, g % 16
            s4 = g % 4
            pkv = pkb[s4 // 2][:, 256 * (s4 % 2):256 * (s4 % 2) + 256]
            pkkeys = [PKK[s4 // 2][s4 % 2]]
            scv = dscl[:, 16 * s4:16 * s4 + 16]
            sckey = f"dscl{s4}"
            kb.op("vector", lambda e, scv=scv: e.tensor_scalar(
                scv, scv, P126, 1.0, OP.mult, OP.mult),
                reads=[sckey], writes=[sckey])
            wn = wnat[g % 3]
            wnk = f"dqwn{g % 3}"
            sbc = scv.unsqueeze(2).broadcast_to([128, 16, 16])
            for lohi in range(2):
                tv = dqt[lohi][:, 256 * (g % 2):256 * (g % 2) + 256]
                dq_nibble(pkv, pkkeys, sckey, sbc,
                          wn[:, 256 * lohi:256 * lohi + 256], wnk + str(lohi),
                          lohi, tv, [DQTK[lohi][g % 2]])
            banks_now = dq_state["banks"]

            def pe_part(u=u, hc=hc, wn=wn, wnk=wnk, banks=banks_now):
                for lohi in range(2):
                    for sub in range(2):
                        sl = 2 * lohi + sub
                        bank = banks[lohi]
                        kb.op("tensor", lambda e, wn=wn, lohi=lohi, sub=sub, bank=bank:
                              e.transpose(
                                  pbf(bank)[:, 0:128],
                                  wn[:, 256 * lohi + 128 * sub:
                                     256 * lohi + 128 * sub + 128],
                                  ident_s[:]),
                              reads=[wnk + str(lohi), "ident"], writes=[pk(bank)])
                        kb.op("vector", lambda e, hc=hc, sl=sl, bank=bank: e.tensor_copy(
                            stage[sl][:, 128 * hc:128 * hc + 128], pbf(bank)[:, 0:128]),
                            reads=[pk(bank)], writes=[f"stage{sl}"])
                if hc == 15:
                    for sl in range(4):
                        kb.dma("sync", wdT_d[4 * u + sl, :, :], stage[sl][:],
                               "wdT_out", reads=[f"stage{sl}"],
                               writes=[f"wdT_{4 * u + sl}"])
            down_state["pend"].append(pe_part)
            if len(down_state["pend"]) > 2:
                down_state["pend"].pop(0)()
            down_state["g"] += 1

    def down_flush():
        for f in down_state["pend"]:
            f()
        down_state["pend"] = []

    def ln_from_sums(csum, csumsq, cs1, cs2):
        t1 = stats[:, cs1:cs1 + 1]
        kb.op("vector", lambda e: e.tensor_scalar(
            stats[:, csum:csum + 1], stats[:, csum:csum + 1], 1.0 / Fd, 1.0,
            OP.mult, OP.mult), reads=[f"st{csum}"], writes=[f"st{csum}"])
        kb.op("vector", lambda e: e.scalar_tensor_tensor(
            t1, stats[:, csum:csum + 1], 1.0, stats[:, csum:csum + 1],
            OP.mult, OP.mult), reads=[f"st{csum}"], writes=[f"st{cs1}"])
        kb.op("vector", lambda e: e.tensor_scalar(
            stats[:, csumsq:csumsq + 1], stats[:, csumsq:csumsq + 1], 1.0 / Fd, 1e-5,
            OP.mult, OP.add), reads=[f"st{csumsq}"], writes=[f"st{csumsq}"])
        kb.op("vector", lambda e: e.tensor_tensor(
            stats[:, csumsq:csumsq + 1], stats[:, csumsq:csumsq + 1], t1,
            OP.subtract), reads=[f"st{csumsq}", f"st{cs1}"], writes=[f"st{csumsq}"])
        kb.op("scalar", lambda e: e.activation(
            t1, stats[:, csumsq:csumsq + 1], AF.Sqrt),
            reads=[f"st{csumsq}"], writes=[f"st{cs1}"])
        kb.op("vector", lambda e: e.reciprocal(
            stats[:, csumsq:csumsq + 1], t1),
            reads=[f"st{cs1}"], writes=[f"st{csumsq}"])

    def ln_stats(xkey, xap, csum, csumsq, cs1, cs2, junk=None, junkkey=("o1",)):
        if isinstance(xkey, str):
            xkey = [xkey]
        jap = o1[:] if junk is None else junk
        junkkey = list(junkkey)
        kb.op("vector", lambda e: e.scalar_tensor_tensor(
            jap, xap, 1.0, xap, OP.mult, OP.max,
            accum_out=stats[:, csum:csum + 1]),
            reads=xkey, writes=junkkey + [f"st{csum}"])
        kb.op("vector", lambda e: e.scalar_tensor_tensor(
            jap, xap, 1.0, xap, OP.mult, OP.mult,
            accum_out=stats[:, csumsq:csumsq + 1]),
            reads=xkey + junkkey, writes=junkkey + [f"st{csumsq}"])
        ln_from_sums(csum, csumsq, cs1, cs2)

    def ln_tile_transposed(seq, trow, dst, keyfmt, colbase, bufi):
        xkey = f"xf{bufi}"
        kb.dma("scalar", xf[bufi][:], x_in[seq, trow:trow + 128, :], f"xfs{bufi}",
               writes=[xkey])
        cb = 4 * bufi
        jk = [("o1",), ("p1b",)][bufi]
        jap = [o1[:], p1b][bufi]
        ln_stats(xkey, xf[bufi][:], cb, cb + 1, cb + 2, cb + 3, jap, jk)
        kb.op("vector", lambda e: e.tensor_scalar(
            p1a, xf[bufi][:], stats[:, cb:cb + 1], stats[:, cb + 1:cb + 2],
            OP.subtract, OP.mult),
            reads=[xkey, f"st{cb}", f"st{cb + 1}"], writes=["p1a"])
        for fc in range(FC):
            bank = 4 + fc % 2
            kb.op("tensor", lambda e, fc=fc, bank=bank: e.transpose(
                pbf(bank)[:, 0:128], p1a[:, 128 * fc:128 * fc + 128], ident_s[:]),
                reads=["p1a", "ident"], writes=[pk(bank)])
            kb.op("vector", lambda e, fc=fc, bank=bank: e.tensor_copy(
                dst[:, 512 * fc + colbase:512 * fc + colbase + 128],
                pbf(bank)[:, 0:128]),
                reads=[pk(bank)], writes=[keyfmt.format(i=fc)])

    def attn_phaseA(seq):
        for sup in range(NSUP):
            kb.new_phase()
            kb.dma("gpsimd", bvrow, bv.unsqueeze(0), "bvrow", writes=BRK)
            for tb in range(TBS):
                ln_tile_transposed(seq, 512 * sup + 128 * tb, lnTs, "fu_{i}",
                                   128 * tb, tb % 2)
            for fc in range(FC):
                kb.dma("sync", lnT_d[128 * fc:128 * fc + 128, 512 * sup:512 * sup + 512],
                       lnTs[:, 512 * fc:512 * fc + 512], "lnT_out",
                       reads=[f"fu_{fc}"], writes=[f"lnTd_{fc}_{sup}"])
            for jc in range(JC):
                sl, slkeys = wslot[jc % 2], qkeys(jc % 2)
                kb.dma("sync", sl, wqkT_d[0, jc], f"wsls{jc % 2}",
                       reads=[f"wb_0_{fc}" for fc in range(FC)], writes=slkeys)
                bank = 6 + jc % 2

                def mm(e, sl=sl, bank=bank):
                    for fc in range(FC):
                        ins = e.matmul(pf(bank), sl[:, 128 * fc:128 * fc + 128],
                                       lnTs[:, 512 * fc:512 * fc + 512],
                                       start=(fc == 0), stop=(fc == FC - 1))
                    return ins
                kb.op("tensor", mm,
                      reads=slkeys + [f"fu_{fc}" for fc in range(FC)],
                      writes=[pk(bank)])
                kt, ktkey = kTs[jc % 2], f"kTs{jc % 2}"
                kb.op("vector", lambda e, bank=bank, jc=jc, kt=kt: e.tensor_scalar(
                    kt[:, 0:512], pf(bank), bk_t[:, jc:jc + 1], 1.0, OP.add, OP.mult),
                    reads=[pk(bank), "bk"], writes=[ktkey])
                kb.dma("scalar", kT_d[jc, :, 512 * sup:512 * sup + 512], kt[:, 0:512],
                       "kT_out", reads=[ktkey], writes=[f"kTd_{jc}_{sup}"])
                down_micro(2, banks=(4, 5))
            for hs in range(4):
                for fc in range(FC):
                    ch, chkeys = wvbuf[fc % 4], WVK[fc % 4]
                    kb.dma("sync" if fc % 2 == 0 else "gpsimd", ch,
                           wqkvo_d[2, 128 * fc:128 * fc + 128, 512 * hs:512 * hs + 512],
                           f"wvs{fc % 4}", reads=[f"wc_2_{fc}"], writes=chkeys)

                    def mmv(e, ch=ch, fc=fc):
                        for tb2 in range(4):
                            ins = e.matmul(
                                pf(tb2),
                                lnTs[:, 512 * fc + 128 * tb2:512 * fc + 128 * tb2 + 128],
                                ch, start=(fc == 0), stop=False)
                        return ins
                    kb.op("tensor", mmv,
                          reads=chkeys + [f"fu_{fc2}" for fc2 in range(FC)],
                          writes=[pk(0), pk(1), pk(2), pk(3)])
                    if fc % 2 == 0:
                        down_micro(1, banks=(4, 5))

                def mmvb(e, hs=hs):
                    for tb2 in range(4):
                        ins = e.matmul(pf(tb2), ones_s[0:1, :],
                                       bvrow[0:1, 512 * hs:512 * hs + 512],
                                       start=False, stop=True)
                    return ins
                kb.op("tensor", mmvb, reads=["ones", f"px{hs}"],
                      writes=[pk(0), pk(1), pk(2), pk(3)])
                for tb2 in range(4):
                    tbg = 4 * sup + tb2
                    dst = bass.AP(vb, 2080 * tbg + 65 * 8 * hs,
                                  [[16 * NHEAD * 65, 128], [65, 8], [1, 64]])
                    kb.op("vector", lambda e, tb2=tb2, dst=dst: e.tensor_copy(
                        dst, pf(tb2).rearrange("p (h d) -> p h d", d=64)),
                        reads=[pk(tb2)], writes=[f"vb_{tbg}_{hs}"])
        down_flush()
        assert down_state["g"] % 16 == 0, down_state["g"]
        kb.new_phase()
        for ts4 in range(4):
            for jc in range(JC):
                kt, ktkey = kTs[jc % 2], f"kTs{jc % 2}"
                kb.dma("sync", kt[:, 0:512], kT_d[jc, :, 512 * ts4:512 * ts4 + 512],
                       f"kTss{jc % 2}", reads=[f"kTd_{jc}_{ts4}"], writes=[ktkey])
                kb.op("vector", lambda e, kt=kt: e.tensor_tensor(
                    p1a[:, 0:512], kt[:, 0:512], kt[:, 0:512], OP.mult),
                    reads=[ktkey], writes=["p1a"])
                kb.op("tensor", lambda e, jc=jc: e.matmul(
                    pf(7)[0:32, :], E_s[:, 32 * jc:32 * jc + 32], p1a[:, 0:512],
                    start=(jc == 0), stop=(jc == JC - 1)),
                    reads=["p1a", "E"], writes=[pk(7)])
            kb.op("vector", lambda e: e.tensor_reduce(
                stats[0:32, 16:17], pf(7)[0:32, :], AX.X, OP.max),
                reads=[pk(7)], writes=["st16"])
            if ts4 == 0:
                kb.op("vector", lambda e: e.tensor_copy(
                    mkq[0:32, 0:1], stats[0:32, 16:17]),
                    reads=["st16"], writes=["maxk2"])
            else:
                kb.op("vector", lambda e: e.tensor_tensor(
                    mkq[0:32, 0:1], mkq[0:32, 0:1], stats[0:32, 16:17], OP.max),
                    reads=["st16", "maxk2"], writes=["maxk2"])

    def attn_phaseB(seq):
        for sup in range(NSUP):
            kb.new_phase()
            for fc in range(FC):
                kb.dma("sync" if fc % 2 == 0 else "scalar",
                       lnTs2[:, 512 * fc:512 * fc + 512],
                       lnT_d[128 * fc:128 * fc + 128, 512 * sup:512 * sup + 512],
                       f"lnT_in{fc % 2}", reads=[f"lnTd_{fc}_{sup}"],
                       writes=[f"l2_{fc}"])
            for jc in range(JC):
                sl, slkeys = wslot[jc % 2], qkeys(jc % 2)
                kb.dma("sync", sl, wqkT_d[1, jc], f"wsls{jc % 2}",
                       reads=[f"wb_1_{fc}" for fc in range(FC)], writes=slkeys)
                bank = 6 + jc % 2

                def mm(e, sl=sl, bank=bank):
                    for fc in range(FC):
                        ins = e.matmul(pf(bank), sl[:, 128 * fc:128 * fc + 128],
                                       lnTs2[:, 512 * fc:512 * fc + 512],
                                       start=(fc == 0), stop=(fc == FC - 1))
                    return ins
                kb.op("tensor", mm,
                      reads=slkeys + [f"l2_{fc}" for fc in range(FC)],
                      writes=[pk(bank)])
                kb.op("vector", lambda e, bank=bank, jc=jc: e.tensor_scalar(
                    qTb[:, 512 * jc:512 * jc + 512], pf(bank),
                    bq_t[:, jc:jc + 1], 1.0, OP.add, OP.mult),
                    reads=[pk(bank), "bq"], writes=[f"qTb_{jc}"])
            for hp0 in range(2):
                kb.dma("sync", kTs[hp0], kT_d[hp0, :, :], f"kTss{hp0}",
                       reads=[f"kTd_{hp0}_{s2}" for s2 in range(4)],
                       writes=[f"kTs{hp0}"])
            for jc in range(JC):
                kb.op("vector", lambda e, jc=jc: e.tensor_tensor(
                    p1a[:, 0:512], qTb[:, 512 * jc:512 * jc + 512],
                    qTb[:, 512 * jc:512 * jc + 512], OP.mult),
                    reads=[f"qTb_{jc}"], writes=["p1a"])
                kb.op("tensor", lambda e, jc=jc: e.matmul(
                    pf(7)[0:32, :], E_s[:, 32 * jc:32 * jc + 32], p1a[:, 0:512],
                    start=(jc == 0), stop=(jc == JC - 1)),
                    reads=["p1a", "E"], writes=[pk(7)])
            kb.op("vector", lambda e: e.tensor_reduce(
                mkq[0:32, 1:2], pf(7)[0:32, :], AX.X, OP.max),
                reads=[pk(7)], writes=["maxq2"])
            kb.op("vector", lambda e: e.tensor_tensor(
                mkq[0:32, 2:3], mkq[0:32, 0:1], mkq[0:32, 1:2], OP.mult),
                reads=["maxk2", "maxq2"], writes=["B2"])
            kb.op("scalar", lambda e: e.activation(
                mkq[0:32, 3:4], mkq[0:32, 2:3], AF.Ln),
                reads=["B2"], writes=["B3"])
            kb.op("scalar", lambda e: e.activation(
                mkq[0:32, 4:5], mkq[0:32, 3:4], AF.Exp, scale=0.5),
                reads=["B3"], writes=["B4"])
            kb.op("vector", lambda e: e.tensor_scalar(
                mkq[0:32, 5:6], mkq[0:32, 4:5], -1.0 / SCALE, 1.0, OP.mult, OP.mult),
                reads=["B4"], writes=["B5"])
            kb.op("vector", lambda e: e.tensor_copy(
                Bcol[0:32, :], mkq[0:32, 5:6]),
                reads=["B5"], writes=["Bcol"])
            kb.op("tensor", lambda e: e.transpose(
                pbf(6)[0:1, 0:32], Bcol[0:32, :], ident_s[0:32, 0:32]),
                reads=["Bcol", "ident"], writes=[pk(6)])
            kb.op("vector", lambda e: e.tensor_copy(
                Brow[0:1, :], pbf(6)[0:1, 0:32]),
                reads=[pk(6)], writes=["Brow"])
            kb.op("tensor", lambda e: e.matmul(
                pf(7)[:, 0:32], ones_s[0:1, :], Brow[0:1, :], start=True, stop=True),
                reads=["Brow", "ones"], writes=[pk(7)])
            kb.op("vector", lambda e: e.tensor_copy(
                negBb[:, :], pf(7)[:, 0:32]),
                reads=[pk(7)], writes=["negBb"])

            for hp in range(HP):
                kt, ktkey = kTs[hp % 2], f"kTs{hp % 2}"
                gu_step(1, banks=(2, 3))
                pend_av = []

                def mk_av(tc, w0=None, w1=None, k0=None, k1=None, hp=hp):
                    def go():
                        vt0 = vb[:, 2080 * tc + 65 * (2 * hp):
                                 2080 * tc + 65 * (2 * hp) + 65]
                        vt1 = vb[:, 2080 * tc + 65 * (2 * hp + 1):
                                 2080 * tc + 65 * (2 * hp + 1) + 65]
                        vkeys = [f"vb_{tc}_{hs}" for hs in range(4)] + ["vbones"]
                        kb.op("tensor", lambda e: e.matmul(
                            pf(4)[0:65, :], vt0, w0, start=(tc == 0),
                            stop=(tc == TCH - 1)),
                            reads=[k0] + vkeys, writes=[pk(4)])
                        kb.op("tensor", lambda e: e.matmul(
                            pf(5)[0:65, :], vt1, w1, start=(tc == 0),
                            stop=(tc == TCH - 1)),
                            reads=[k1] + vkeys, writes=[pk(5)])
                    return go

                for tc in range(TCH):
                    b0, b1 = (0, 1) if tc % 2 == 0 else (2, 3)
                    w0, w1 = pex[(2 * tc) % 8], pex[(2 * tc + 1) % 8]
                    k0, k1 = PXK[(2 * tc) % 8], PXK[(2 * tc + 1) % 8]
                    kb.op("tensor", lambda e, kt=kt, tc=tc, b0=b0, hp=hp: e.matmul(
                        pf(b0), kt[0:64, 128 * tc:128 * tc + 128],
                        qTb[0:64, 512 * hp:512 * hp + 512],
                        start=True, stop=True, tile_position=(0, 0)),
                        reads=[ktkey, f"qTb_{hp}"], writes=[pk(b0)])
                    kb.op("tensor", lambda e, kt=kt, tc=tc, b1=b1, hp=hp: e.matmul(
                        pf(b1), kt[64:128, 128 * tc:128 * tc + 128],
                        qTb[64:128, 512 * hp:512 * hp + 512],
                        start=True, stop=True, tile_position=(64, 0)),
                        reads=[ktkey, f"qTb_{hp}"], writes=[pk(b1)])
                    if len(pend_av) == 2:
                        pend_av.pop(0)()
                    kb.op("scalar", lambda e, w0=w0, hp=hp, b0=b0: e.activation(
                        w0, pf(b0), AF.Exp, bias=negBb[:, 2 * hp:2 * hp + 1],
                        scale=1.0 / SCALE),
                        reads=[pk(b0), "negBb"], writes=[k0])
                    kb.op("scalar", lambda e, w1=w1, hp=hp, b1=b1: e.activation(
                        w1, pf(b1), AF.Exp, bias=negBb[:, 2 * hp + 1:2 * hp + 2],
                        scale=1.0 / SCALE),
                        reads=[pk(b1), "negBb"], writes=[k1])
                    pend_av.append(mk_av(tc, w0=w0, w1=w1, k0=k0, k1=k1))
                for go in pend_av:
                    go()
                pend_av = []
                kb.op("vector", lambda e, hp=hp: e.tensor_copy(
                    fUT[0:64, 512 * hp:512 * hp + 512], pf(4)[0:64, :]),
                    reads=[pk(4)], writes=[f"fu_{hp}"])
                kb.op("vector", lambda e, hp=hp: e.tensor_copy(
                    fUT[64:128, 512 * hp:512 * hp + 512], pf(5)[0:64, :]),
                    reads=[pk(5), f"fu_{hp}"], writes=[f"fu_{hp}"])
                kb.op("vector", lambda e: e.tensor_copy(
                    p1a[64:65, 0:512], pf(4)[64:65, :]),
                    reads=[pk(4)], writes=["p1a"])
                kb.op("vector", lambda e: e.tensor_copy(
                    p1a[64:65, 512:1024], pf(5)[64:65, :]),
                    reads=[pk(5), "p1a"], writes=["p1a"])
                for half in range(2):
                    h = 2 * hp + half
                    for sc in range(4):
                        kb.op("tensor", lambda e, half=half, sc=sc, h=h: e.transpose(
                            pbf(6)[:, 2 * (32 * sc + h):2 * (32 * sc + h) + 1],
                            p1a[64:65, 512 * half + 128 * sc:
                                512 * half + 128 * sc + 128],
                            ident_s[64:65, 64:65]),
                            reads=["p1a", "ident"], writes=[pk(6)])
                if hp + 2 < HP:
                    kb.dma("sync", kTs[hp % 2], kT_d[hp + 2, :, :], f"kTss{hp % 2}",
                           reads=[f"kTd_{hp + 2}_{s2}" for s2 in range(4)],
                           writes=[ktkey])
            kb.op("vector", lambda e: e.reciprocal(
                ZtR[:, 0:128], pbf(6)[:, 0:256:2]),
                reads=[pk(6)], writes=["ZtR"])
            for sc in range(4):
                for hp in range(HP):
                    bank = 6 + hp % 2
                    kb.op("tensor", lambda e, hp=hp, sc=sc, bank=bank: e.transpose(
                        pbf(bank)[:, 0:128],
                        fUT[:, 512 * hp + 128 * sc:512 * hp + 128 * sc + 128],
                        ident_s[:]),
                        reads=[f"fu_{hp}", "ident"], writes=[pk(bank)])
                    for half in range(2):
                        h = 2 * hp + half
                        kb.op("vector", lambda e, h=h, sc=sc, half=half, bank=bank:
                              e.tensor_tensor(
                                  p1d[:, 64 * h:64 * h + 64],
                                  pbf(bank)[:, 64 * half:64 * half + 64],
                                  ZtR[:, 32 * sc + h:32 * sc + h + 1].broadcast_to(
                                      [128, 64]),
                                  OP.mult),
                              reads=[pk(bank), "ZtR"], writes=[PXK[4 + h // 8]])
                for jc in range(JC):
                    bank = 6 + jc % 2
                    kb.op("tensor", lambda e, jc=jc, bank=bank: e.transpose(
                        pbf(bank)[:, 0:128], p1d[:, 128 * jc:128 * jc + 128],
                        ident_s[:]),
                        reads=[PXK[4 + jc // 4], "ident"], writes=[pk(bank)])
                    kb.op("vector", lambda e, jc=jc, sc=sc, bank=bank: e.tensor_copy(
                        lnTs2[:, 512 * jc + 128 * sc:512 * jc + 128 * sc + 128],
                        pbf(bank)[:, 0:128]),
                        reads=[pk(bank)], writes=[f"l2_{jc}"])
            attS = [p1b, p1d]
            attK = [["p1b"], P1DK]
            kb.dma("gpsimd", borow, bo.unsqueeze(0), "borow", writes=BRK)
            for sbp in range(2):
                for jc in range(JC):
                    wo, wokeys = wslot[2 + jc % 2], qkeys(2 + jc % 2)
                    kb.dma("sync" if jc % 2 == 0 else "gpsimd", wo,
                           wqkvo_d[3, 128 * jc:128 * jc + 128, :],
                           f"wos{jc % 2}", reads=[f"wc_3_{jc}"], writes=wokeys)

                    def mmo(e, wo=wo, jc=jc, sbp=sbp):
                        for i in range(2):
                            sb_ = 2 * sbp + i
                            for js in range(4):
                                ins = e.matmul(
                                    pf(2 * js + i),
                                    lnTs2[:, 512 * jc + 128 * sb_:
                                          512 * jc + 128 * sb_ + 128],
                                    wo[:, 512 * js:512 * js + 512],
                                    start=(jc == 0), stop=False)
                        return ins
                    kb.op("tensor", mmo,
                          reads=wokeys + [f"l2_{jc2}" for jc2 in range(JC)],
                          writes=[pk(b) for b in range(8)])

                def mmob(e):
                    for i in range(2):
                        for js in range(4):
                            ins = e.matmul(pf(2 * js + i), ones_s[0:1, :],
                                           borow[0:1, 512 * js:512 * js + 512],
                                           start=False, stop=True)
                    return ins
                kb.op("tensor", mmob, reads=["ones"] + BRK,
                      writes=[pk(b) for b in range(8)])
                for i in range(2):
                    for js in range(4):
                        kb.op("vector", lambda e, i=i, js=js: e.tensor_scalar(
                            attS[i][:, 512 * js:512 * js + 512], pf(2 * js + i),
                            1.0, 1.0, OP.mult, OP.mult),
                            reads=[pk(2 * js + i)], writes=attK[i])
                for i in range(2):
                    sb_ = 2 * sbp + i
                    ln_stats(attK[i], attS[i], 16, 17, 18, 19)
                    kb.op("vector", lambda e: e.scalar_tensor_tensor(
                        stats[:, 20:21], stats[:, 16:17], 1.0, stats[:, 17:18],
                        OP.mult, OP.mult),
                        reads=["st16", "st17"], writes=["st20"])
                    xkey = f"xf{i}"
                    trow = 512 * sup + 128 * sb_
                    kb.dma("gpsimd", xf[i][:], x_in[seq, trow:trow + 128, :],
                           f"xfs{i}", writes=[xkey])
                    kb.op("vector", lambda e, i=i: e.tensor_scalar(
                        o1[:], xf[i][:], stats[:, 20:21], 1.0, OP.subtract, OP.mult),
                        reads=[xkey, "st20"], writes=["o1"])
                    kb.op("vector", lambda e, i=i: e.scalar_tensor_tensor(
                        xf[i][:], attS[i], stats[:, 17:18], o1[:], OP.mult, OP.add),
                        reads=attK[i] + ["st17", "o1"], writes=[xkey])
                    kb.dma("scalar", x1_d[2048 * seq + trow:2048 * seq + trow + 128, :],
                           xf[i][:], "x1_out", reads=[xkey],
                           writes=[f"x1d_{(2048 * seq + trow) // 128}"])

    def ffn_tg(tg):
        kb.new_phase()
        rowbase = 512 * tg
        for tb in range(TBS):
            bi = tb % 2
            xkey = f"xf{bi}"
            kb.dma("sync", xf[bi][:], x1_d[rowbase + 128 * tb:rowbase + 128 * tb + 128, :],
                   f"xfs{bi}", reads=[f"x1d_{(rowbase + 128 * tb) // 128}"],
                   writes=[xkey])
            cb = 4 * bi
            jk = [("o1",), ("p1b",)][bi]
            jap = [o1[:], p1b][bi]
            ln_stats(xkey, xf[bi][:], cb, cb + 1, cb + 2, cb + 3, jap, jk)
            kb.op("vector", lambda e, bi=bi, cb=cb: e.tensor_scalar(
                p1a[:, 0:1024], xf[bi][:, 0::2], stats[:, cb:cb + 1],
                stats[:, cb + 1:cb + 2], OP.subtract, OP.mult),
                reads=[xkey, f"st{cb}", f"st{cb + 1}"], writes=["p1a"])
            kb.op("vector", lambda e, bi=bi, cb=cb: e.tensor_scalar(
                p1a[:, 1024:2048], xf[bi][:, 1::2], stats[:, cb:cb + 1],
                stats[:, cb + 1:cb + 2], OP.subtract, OP.mult),
                reads=[xkey, f"st{cb}", f"st{cb + 1}", "p1a"], writes=["p1a"])
            for fc in range(FC):
                bank = 4 + fc % 2
                kb.op("tensor", lambda e, fc=fc, bank=bank: e.transpose(
                    pbf(bank)[:, 0:128], p1a[:, 128 * fc:128 * fc + 128], ident_s[:]),
                    reads=["p1a", "ident"], writes=[pk(bank)])
                kb.op("vector", lambda e, fc=fc, tb=tb, bank=bank: e.tensor_copy(
                    ln2Ts[:, 512 * fc + 128 * tb:512 * fc + 128 * tb + 128],
                    pbf(bank)[:, 0:128]),
                    reads=[pk(bank)], writes=[f"ln2_{fc}"])
        for nh in range(2):
            for c in range(NCH):
                gc = NCH * nh + c
                slb, slkeys = wgu[c % 2], WGUK[c % 2]
                kb.dma("sync", slb.rearrange("p (a b) -> p a b", a=2),
                       wguT_d[gc].rearrange("p (a b) -> p a b", a=2),
                       f"wgus{c % 2}", reads=[f"wguT_{gc}"], writes=slkeys)
                bg, bu = [(0, 1), (2, 3), (4, 5)][c % 3]

                def mmg(e, slb=slb, bg=bg):
                    for fc in range(FC):
                        ins = e.matmul(pf(bg), slb[:, 128 * fc:128 * fc + 128],
                                       ln2Ts[:, 512 * fc:512 * fc + 512],
                                       start=(fc == 0), stop=(fc == FC - 1))
                    return ins
                kb.op("tensor", mmg,
                      reads=slkeys + [f"ln2_{fc}" for fc in range(FC)],
                      writes=[pk(bg)])

                def mmu(e, slb=slb, bu=bu):
                    for fc in range(FC):
                        ins = e.matmul(pf(bu), slb[:, 2048 + 128 * fc:2048 + 128 * fc + 128],
                                       ln2Ts[:, 512 * fc:512 * fc + 512],
                                       start=(fc == 0), stop=(fc == FC - 1))
                    return ins
                kb.op("tensor", mmu,
                      reads=slkeys + [f"ln2_{fc}" for fc in range(FC)],
                      writes=[pk(bu)])
                kb.op("scalar", lambda e, bg=bg: e.copy(pex[0], pf(bg)),
                      reads=[pk(bg)], writes=["px0"])
                kb.op("vector", lambda e: e.tensor_tensor(
                    pex[1], pex[0], pex[0], OP.mult),
                    reads=["px0"], writes=["px1"])
                kb.op("vector", lambda e: e.tensor_scalar(
                    pex[2], pex[1], GC1, GC0, OP.mult, OP.add),
                    reads=["px1"], writes=["px2"])
                kb.op("vector", lambda e: e.tensor_tensor(
                    pex[3], pex[2], pex[0], OP.mult),
                    reads=["px2", "px0"], writes=["px3"])
                kb.op("scalar", lambda e: e.activation(
                    pex[1], pex[3], AF.Sigmoid),
                    reads=["px3"], writes=["px1"])
                kb.op("vector", lambda e: e.tensor_tensor(
                    pex[2], pex[1], pex[0], OP.mult),
                    reads=["px1", "px0"], writes=["px2"])
                kb.op("vector", lambda e, c=c, bu=bu: e.tensor_tensor(
                    interT[:, 512 * c:512 * c + 512], pex[2], pf(bu), OP.mult),
                    reads=["px2", pk(bu)], writes=[f"it_{c}"])
            for hh in range(2):
                for c in range(NCH):
                    gc = NCH * nh + c
                    dnb, dnkeys = wdn[c % 4], WDNK[c % 4]
                    kb.dma("sync", dnb, wdT_d[gc][:, 1024 * hh:1024 * hh + 1024],
                           f"wdns{c % 4}", reads=[f"wdT_{gc}"], writes=dnkeys)

                    def mmd(e, dnb=dnb, c=c):
                        for tb in range(4):
                            for hs in range(2):
                                ins = e.matmul(
                                    pf(2 * tb + hs),
                                    interT[:, 512 * c + 128 * tb:512 * c + 128 * tb + 128],
                                    dnb[:, 512 * hs:512 * hs + 512],
                                    start=(c == 0), stop=(c == NCH - 1))
                        return ins
                    kb.op("tensor", mmd,
                          reads=dnkeys + [f"it_{c}"],
                          writes=[pk(i) for i in range(8)])
                for tb in range(4):
                    kb.op("vector", lambda e, tb=tb: e.tensor_copy(
                        o1[:, 0:512], pf(2 * tb)),
                        reads=[pk(2 * tb)], writes=["o1"])
                    kb.op("vector", lambda e, tb=tb: e.tensor_copy(
                        o1[:, 512:1024], pf(2 * tb + 1)),
                        reads=[pk(2 * tb + 1), "o1"], writes=["o1"])
                    r0 = rowbase + 128 * tb
                    if nh == 0:
                        kb.dma("sync", x2p_d[r0:r0 + 128, 1024 * hh:1024 * hh + 1024],
                               o1[:, 0:1024], f"x2p_out{tb % 2}", reads=["o1"],
                               writes=[f"x2p_{tb}_{hh}_{tg}"])
                    else:
                        kb.dma("gpsimd", x2p_d[r0:r0 + 128, 1024 * hh:1024 * hh + 1024],
                               o1[:, 0:1024], f"x2pa{tb % 2}",
                               reads=["o1", f"x2p_{tb}_{hh}_{tg}"],
                               writes=[f"x2p_{tb}_{hh}_{tg}"],
                               accum_op=OP.add)
        for tb in range(TBS):
            r0 = rowbase + 128 * tb
            kb.dma("sync", xf[0][:], x2p_d[r0:r0 + 128, :], "x2p_in",
                   reads=[f"x2p_{tb}_{hh}_{tg}" for hh in range(2)], writes=["xf0"])
            kb.dma("sync", xf[1][:], x1_d[r0:r0 + 128, :], "x1_in2",
                   reads=[f"x1d_{r0 // 128}"], writes=["xf1"])
            ln_stats("xf0", xf[0][:], 16, 17, 18, 19)
            kb.op("vector", lambda e: e.scalar_tensor_tensor(
                stats[:, 20:21], stats[:, 16:17], 1.0, stats[:, 17:18],
                OP.mult, OP.mult),
                reads=["st16", "st17"], writes=["st20"])
            kb.op("vector", lambda e: e.tensor_scalar(
                o1[:], xf[1][:], stats[:, 20:21], 1.0, OP.subtract, OP.mult),
                reads=["xf1", "st20"], writes=["o1"])
            kb.op("vector", lambda e: e.scalar_tensor_tensor(
                xf[1][:], xf[0][:], stats[:, 17:18], o1[:], OP.mult, OP.add),
                reads=["xf0", "st17", "o1"], writes=["xf1"])
            seq = r0 // 2048
            srow = r0 % 2048
            kb.dma("sync", out[seq, srow:srow + 128, :], xf[1][:], "outp",
                   reads=["xf1"], writes=[f"out_{r0 // 128}"])

    for seq in range(BC):
        attn_phaseA(seq)
        attn_phaseB(seq)
        gu_flush(banks=(2, 3))
    assert down_state["g"] == 512, down_state["g"]
    assert gu_state["pe"] == N_GU and gu_state["dve"] == N_GU, gu_state
    kb.full_barrier()
    kb.new_phase()
    for tg in range(TG):
        ffn_tg(tg)
    kb.finalize(done_scr[:])

    with nc.Block() as block:
        @block.tensor
        def _(e):
            for f in kb.stream["tensor"]:
                f(e)

        @block.vector
        def _(e):
            for f in kb.stream["vector"]:
                f(e)

        @block.scalar
        def _(e):
            for f in kb.stream["scalar"]:
                f(e)

        @block.gpsimd
        def _(e):
            for f in kb.stream["gpsimd"]:
                f(e)

        @block.sync
        def _(e):
            for f in kb.stream["sync"]:
                f(e)

    return nc


_NC_CACHE = {}


def kernel(**inputs):
    x = np.ascontiguousarray(inputs["L0_attn_x"])
    if "nc" not in _NC_CACHE:
        _NC_CACHE["nc"] = build_nc()
    nc = _NC_CACHE["nc"]
    wkeys = ["L0_attn_Wq", "L0_attn_bq", "L0_attn_Wk", "L0_attn_bk",
             "L0_attn_Wv", "L0_attn_bv", "L0_attn_Wo", "L0_attn_bo",
             "L0_ffn_gate_W_packed", "L0_ffn_gate_scales",
             "L0_ffn_up_W_packed", "L0_ffn_up_scales",
             "L0_ffn_down_W_packed", "L0_ffn_down_scales"]
    shared = {k: np.ascontiguousarray(inputs[k]) for k in wkeys}
    in_maps = []
    for c in range(8):
        m = dict(shared)
        m["L0_attn_x"] = np.ascontiguousarray(x[2 * c:2 * c + 2])
        in_maps.append(m)
    res = run_bass_kernel_spmd(nc, in_maps, core_ids=list(range(8)))
    return np.concatenate([r["out"] for r in res.results], axis=0)
